# version 03
# baseline (speedup 1.0000x reference)
import contextlib
import numpy as np
import ml_dtypes
import concourse.bass as bass
import concourse.mybir as mybir
from concourse.bass_utils import run_bass_kernel_spmd

F32 = mybir.dt.float32
BF16 = mybir.dt.bfloat16
I32 = mybir.dt.int32
AF = mybir.ActivationFunctionType
ALU = mybir.AluOpType
AX = mybir.AxisListType

D = 1024
NH = 16
HD = 64
TOK = 4096
TA = 512
NTA = TOK // TA
TB = 256
NTB = TOK // TB
HID = 3072
NKI = D // 128
NHC = HID // 128
EPS = 1e-5
QSCALE = 1.0 / 8.0
DQ = 1.0 / 127.0

DEBUG_DUMP = None
PHASES = "AB"
STAGE = 7


class MockInst:
    def then_inc(self, *a, **k):
        return self


class MockEng:
    def __getattr__(self, name):
        def f(*a, **k):
            return MockInst()
        return f


DMA_SEMS = {"IN", "WC", "WT", "WU", "ST"}


class Reg:

    def __init__(self):
        self.phase = 0
        self.cnt = {}
        self.miles = {}
        self.pending = {}
        self.totals = None
        self.sems = None
        self.eng_for_sem = {}

    def start_pass1(self, sems):
        for sem, keys in self.pending.items():
            for key in keys:
                self.miles[(sem, key)] = self.cnt.get(sem, 0)
        self.pending = {}
        self.phase = 1
        self.sems = sems
        self.totals = dict(self.cnt)
        self.cnt = {}

    def inc(self, inst, sem, key=None, amt=1):
        self.cnt[sem] = self.cnt.get(sem, 0) + amt
        if self.phase == 0:
            if key is not None:
                assert (sem, key) not in self.miles and \
                    key not in self.pending.get(sem, []), (sem, key)
                if sem in DMA_SEMS:
                    self.pending.setdefault(sem, []).append(key)
                else:
                    self.miles[(sem, key)] = self.cnt[sem]
        else:
            eng = self.eng_for_sem.get(sem)
            if eng is not None:
                eng.drain().then_inc(self.sems[sem], amt)
            else:
                inst.then_inc(self.sems[sem], amt)
        return inst

    def alias(self, sem, key):
        if self.phase == 0:
            assert (sem, key) not in self.miles, (sem, key)
            self.miles[(sem, key)] = self.cnt.get(sem, 0)

    def stable(self, *sems_):
        if self.phase != 0:
            return
        for sem in sems_:
            for key in self.pending.pop(sem, []):
                self.miles[(sem, key)] = self.cnt.get(sem, 0)

    def qbarrier(self, eng, sem):
        if self.phase == 0:
            self.stable(sem)
        else:
            eng.wait_ge(self.sems[sem], self.cnt.get(sem, 0))

    def wait(self, eng, sem, key):
        if self.phase == 0:
            return
        val = self.miles[(sem, key)]
        assert val <= self.totals.get(sem, 0), ("wait exceeds total", sem, key, val, self.totals.get(sem, 0))
        eng.wait_ge(self.sems[sem], val)

    def final_wait(self, eng, sem):
        if self.phase == 0:
            return
        tot = self.totals.get(sem, 0)
        if tot:
            eng.wait_ge(self.sems[sem], tot)


def bcast(ap, shape):
    return ap.broadcast_to(shape)


def build_program(debug_dump=None, phases="AB"):
    nc = bass.Bass("TRN2", target_bir_lowering=False, debug=False,
                   detect_race_conditions=False)

    xh = nc.dram_tensor("xh", [NTA, 128, 5 * D], F32, kind="ExternalInput").ap()
    wq = nc.dram_tensor("wq", [D, D], F32, kind="ExternalInput").ap()
    wk = nc.dram_tensor("wk", [D, D], F32, kind="ExternalInput").ap()
    wv = nc.dram_tensor("wv", [D, D], F32, kind="ExternalInput").ap()
    wo = nc.dram_tensor("wo", [D, D], F32, kind="ExternalInput").ap()
    wg = nc.dram_tensor("wg", [HID, D], I32, kind="ExternalInput").ap()
    wu = nc.dram_tensor("wu", [HID, D], I32, kind="ExternalInput").ap()
    wd = nc.dram_tensor("wd", [D, HID], I32, kind="ExternalInput").ap()
    ident_in = nc.dram_tensor("ident_in", [128, 128], BF16, kind="ExternalInput").ap()
    consts_in = nc.dram_tensor("consts_in", [128, 8], F32, kind="ExternalInput").ap()
    mask_first_in = nc.dram_tensor("mask_first_in", [128, 256], BF16, kind="ExternalInput").ap()
    mask_std_in = nc.dram_tensor("mask_std_in", [128, 256], BF16, kind="ExternalInput").ap()

    out = nc.dram_tensor("out", [NTB, 128, 2 * D], F32, kind="ExternalOutput").ap()
    dbg = None
    if debug_dump:
        ddt = F32 if debug_dump == "y" else BF16
        dbg = nc.dram_tensor("dbg", [NTA, 128, 5 * D], ddt, kind="ExternalOutput").ap()

    ybuf = nc.dram_tensor("ybuf", [NTA, 128, 4 * D], F32, kind="Internal").ap()
    wbf = {n: nc.dram_tensor(n + "_bf", list(shape), BF16, kind="Internal").ap()
           for n, shape in [("wq", (D, D)), ("wk", (D, D)), ("wv", (D, D)), ("wo", (D, D)),
                            ("wg", (HID, D)), ("wu", (HID, D)), ("wd", (D, HID))]}

    R = Reg()

    es = contextlib.ExitStack()
    with es:
        sbp = lambda name, shape, dt=F32: es.enter_context(nc.sbuf_tensor(name, shape, dt))
        ident = sbp("ident", [128, 128], BF16)
        mask_first = sbp("mask_first", [128, 256], BF16)
        mask_std = sbp("mask_std", [128, 256], BF16)
        ones_bf = sbp("ones_bf", [128, 1], BF16)
        waste = sbp("waste", [128, 8], F32)
        bn12 = sbp("bn12", [128, 12], F32)
        cvals = sbp("cvals", [128, 8], F32)
        nc.const_aps.register(F32, EPS, cvals[:, 0:1])
        nc.const_aps.register(F32, EPS * 127.0 * 127.0, cvals[:, 1:2])
        nc.const_aps.register(F32, float(np.log(DQ)), cvals[:, 2:3])
        semnames = ["IN", "WC", "WT", "WU", "PE", "DV", "AC", "ST"]

        def emit_phaseA(S, engines):
            sync, scalar, vector, tensor, gpsimd = engines

            if gpsimd is not None:
                g = gpsimd
                for i, (src, dst) in enumerate([(wq, wbf["wq"]), (wk, wbf["wk"]),
                                                (wv, wbf["wv"]), (wo, wbf["wo"])]):
                    R.inc(g.dma_start(dst, src), "WC", ("attn_cast", i), 16)
                    R.qbarrier(g, "WC")
                for i, (src, dst) in enumerate([(wg, wbf["wg"]), (wu, wbf["wu"]), (wd, wbf["wd"])]):
                    R.inc(g.dma_start(dst, src), "WC", ("ffn_cast", i), 16)
                    R.qbarrier(g, "WC")

            if sync is not None:
                s = sync
                R.inc(s.dma_start(ident[:], ident_in), "IN", ("const", 0), 16)
                R.inc(s.dma_start(mask_first[:], mask_first_in), "IN", ("const", 1), 16)
                R.inc(s.dma_start(mask_std[:], mask_std_in), "IN", ("const", 2), 16)
                R.inc(s.dma_start(cvals[:], consts_in), "IN", ("const", 3), 16)
                for i, (name, dstT) in enumerate([("wq", S["wqT"]), ("wk", S["wkT"]),
                                                  ("wv", S["wvT"]), ("wo", S["woT"])]):
                    R.qbarrier(s, "IN")
                    R.qbarrier(s, "WT")
                    R.wait(s, "WC", ("attn_cast", i))
                    for ki in range(NKI):
                        R.inc(s.dma_start(dstT[:, ki * D:(ki + 1) * D],
                                          wbf[name][:, ki * 128:(ki + 1) * 128], transpose=True),
                              "WT", ("attn_wT", i, ki), 16)
                for t in range(NTA):
                    if t >= 2:
                        R.qbarrier(s, "IN")
                        if STAGE >= 7:
                            R.wait(s, "DV", ("y_done", t - 2))
                    R.inc(s.dma_start(S["xt"][t % 2][:], xh[t]), "IN", ("x", t), 16)

            for t in range(NTA):
                emit_A_tile(S, engines, t)

        def emit_A_tile(S, engines, t):
            sync, scalar, vector, tensor, gpsimd = engines
            xt = S["xt"][t % 2]
            first_chunk_tile = (t == 0)

            if vector is not None:
                v = vector
                if t == 0:
                    R.wait(v, "IN", ("const", 2))
                    v.memset(S["psE"][:, 0:32], 1.0)
                    R.inc(v.memset(ones_bf[:], 1.0), "DV", ("ones",), 1)
                R.wait(v, "IN", ("x", t))
                if t >= 1:
                    R.wait(v, "PE", ("hT_done", t - 1))
                for b in range(5):
                    v.bn_stats(bn12[:, 0:6], xt[:, b * D:b * D + 512])
                    v.bn_stats(bn12[:, 6:12], xt[:, b * D + 512:(b + 1) * D])
                    v.drain()
                    v.bn_aggr(S["st5"][:, 2 * b:2 * b + 2], bn12[:])
                v.drain()
                R.inc(v.tensor_copy(S["stt"][:, 0:1], S["st5"][:, 0:1]), "DV", ("var1", t), 1)
                R.wait(v, "AC", ("rstd1", t))
                ins = None
                for b in range(5):
                    ins = v.tensor_scalar(S["h"][:, b * D:(b + 1) * D], xt[:, b * D:(b + 1) * D],
                                          S["st5"][:, 2 * b:2 * b + 1], S["st5"][:, 10 + b:11 + b],
                                          ALU.subtract, ALU.mult)
                R.inc(ins, "DV", ("h", t), 1)
                if STAGE < 2:
                    return
                for b in range(5):
                    R.wait(v, "PE", ("hTb", t, b))
                    dst = S["hT"][:].rearrange("p (k n) -> p k n", n=640)[:, :, b * 128:(b + 1) * 128]
                    R.inc(v.tensor_copy(dst, S["psD"][:].rearrange("p (k n) -> p k n", n=128)),
                          "DV", ("hTe", t, b), 1)
                if STAGE < 3:
                    return
                for b in range(5):
                    for oh in range(2):
                        R.wait(v, "PE", ("vmm", t, b, oh))
                        R.inc(v.tensor_copy(S["vtm"][:, b * D + oh * 512: b * D + (oh + 1) * 512],
                                            S["psA"][:, oh * 512:(oh + 1) * 512]),
                              "DV", ("vev", t, b, oh), 1)
                if STAGE < 4:
                    return
                for qc in range(4):
                    for g4 in range(4):
                        gid = qc * 4 + g4
                        R.wait(v, "AC", ("exp", t, qc, g4))
                        if (t, gid) > (0, 1) and STAGE >= 5:
                            pq, pg = ((qc, g4 - 2) if g4 >= 2 else (qc - 1, g4 + 2))
                            if pq >= 0:
                                R.wait(v, "PE", ("avg", t, pq, pg))
                            elif t > 0:
                                R.wait(v, "PE", ("avg", t - 1, 3, pg))
                        mk = mask_first if (first_chunk_tile and qc == 0) else mask_std
                        mk4 = mk[:].rearrange("p (j q) -> p j q", j=2).unsqueeze(1)
                        pr = S["p_raw"][gid % 2]
                        pm = S["p_sb"][gid % 2]
                        R.inc(v.tensor_tensor(pm[:].rearrange("p (g j q) -> p g j q", g=4, j=2),
                                              pr[:].rearrange("p (g j q) -> p g j q", g=4, j=2),
                                              bcast(mk4, [128, 4, 2, 128]), ALU.mult),
                              "DV", ("mask", t, qc, g4), 1)
                    if STAGE < 5:
                        continue
                    R.wait(v, "PE", ("avdone", t, qc))
                    psE_v = S["psE"][:, 0:32].rearrange("p (h k) -> p h k", k=2)[:, :, 0:1]
                    v.reciprocal(S["rs"][:, 0:16].unsqueeze(2), psE_v)
                    v.drain()
                    ins = None
                    for bk in range(2):
                        rs3 = bcast(S["rs"][:, bk * 8:(bk + 1) * 8].unsqueeze(2), [128, 8, 64])
                        ins = v.tensor_tensor(
                            S["a_sb"][:, bk * 512:(bk + 1) * 512].rearrange("p (h d) -> p h d", d=64),
                            S["psA"][:, bk * 512:(bk + 1) * 512].rearrange("p (h d) -> p h d", d=64),
                            rs3, ALU.mult)
                    R.inc(ins, "DV", ("aev", t, qc), 1)
                    R.wait(v, "PE", ("aTt", t, qc))
                    dst = S["aT"][:].rearrange("p (k n) -> p k n", n=TA)[:, :, qc * 128:(qc + 1) * 128]
                    R.inc(v.tensor_copy(dst, S["psD"][:].rearrange("p (k n) -> p k n", n=128)),
                          "DV", ("aTe", t, qc), 1)
                if STAGE < 7:
                    return
                if t >= 1:
                    R.wait(v, "ST", ("y_st", t - 1))
                for qc in range(4):
                    pso = S["psB"] if qc % 2 == 0 else S["psC"]
                    R.wait(v, "PE", ("wo", t, qc))
                    v.bn_stats(bn12[:, 0:6], pso[:, 0:512])
                    v.bn_stats(bn12[:, 6:12], pso[:, 512:1024])
                    v.drain()
                    R.inc(v.bn_aggr(S["stt"][:, 1:3], bn12[:]), "DV", ("var2", t, qc), 1)
                    R.wait(v, "AC", ("rstd2", t, qc))
                    v.tensor_scalar(S["y_sb"][:, qc * D:(qc + 1) * D], pso[:],
                                    S["stt"][:, 1:2], S["stt"][:, 5:6], ALU.subtract, ALU.mult)
                    ins = v.tensor_tensor(S["y_sb"][:, qc * D:(qc + 1) * D],
                                          S["y_sb"][:, qc * D:(qc + 1) * D],
                                          xt[:, (1 + qc) * D:(2 + qc) * D], ALU.add)
                    R.inc(ins, "DV", ("y", t, qc), 1)
                R.alias("DV", ("y_done", t))

            if scalar is not None:
                sc = scalar
                if t == 0:
                    R.wait(sc, "IN", ("const", 3))
                    sc.activation(waste[:], cvals[:, 0:8], AF.Square)
                    sc.drain()
                R.wait(sc, "DV", ("var1", t))
                if t >= 1:
                    R.wait(sc, "DV", ("h", t - 1))
                var_v = S["st5"][:, 0:10].rearrange("p (b k) -> p b k", k=2)[:, :, 1:2]
                rstd_v = S["st5"][:, 10:15].unsqueeze(2)
                sc.activation(rstd_v, var_v, AF.Ln, bias=EPS)
                R.inc(sc.activation(rstd_v, rstd_v, AF.Exp, scale=-0.5),
                      "AC", ("rstd1", t), 1)
                if STAGE < 3:
                    return
                for oc in range(NKI):
                    R.wait(sc, "PE", ("qmm", t, oc))
                    sc.activation(S["qTz"][0][:, oc * TA:(oc + 1) * TA],
                                  S["psA"][:, (oc % 2) * 512:(oc % 2 + 1) * 512],
                                  AF.Copy, scale=cvals[:, 4:5])
                    R.inc(sc.activation(S["qTz"][1][:, oc * TA:(oc + 1) * TA],
                                        S["psA"][:, (oc % 2) * 512:(oc % 2 + 1) * 512],
                                        AF.Copy, scale=cvals[:, 5:6]),
                          "AC", ("qev", t, oc), 1)
                for oc in range(NKI):
                    for part, (c0, cn) in enumerate([(0, 512), (512, 128)]):
                        R.wait(sc, "PE", ("kmm", t, oc, part))
                        R.inc(sc.activation(S["kT"][:, oc * 640 + c0: oc * 640 + c0 + cn],
                                            S["psB"][:, c0:c0 + cn], AF.Copy),
                              "AC", ("kev", t, oc, part), 1)
                if STAGE < 4:
                    return
                for qc in range(4):
                    for g4 in range(4):
                        gid = qc * 4 + g4
                        R.wait(sc, "PE", ("sT", t, qc, g4))
                        if (t, gid) > (0, 1):
                            pq, pg = ((qc, g4 - 2) if g4 >= 2 else (qc - 1, g4 + 2))
                            if pq >= 0:
                                R.wait(sc, "DV", ("mask", t, pq, pg))
                            elif t > 0:
                                R.wait(sc, "DV", ("mask", t - 1, 3, pg))
                        psS = S["psB"] if gid % 2 == 0 else S["psC"]
                        pr = S["p_raw"][gid % 2]
                        sc.activation(pr[:, 0:512], psS[:, 0:512], AF.Exp)
                        R.inc(sc.activation(pr[:, 512:1024], psS[:, 512:1024], AF.Exp),
                              "AC", ("exp", t, qc, g4), 1)
                if STAGE < 7:
                    return
                for qc in range(4):
                    R.wait(sc, "DV", ("var2", t, qc))
                    if qc == 0 and t >= 1:
                        R.wait(sc, "DV", ("y", t - 1, 3))
                    sc.activation(S["stt"][:, 5:6], S["stt"][:, 2:3], AF.Ln, bias=EPS)
                    R.inc(sc.activation(S["stt"][:, 5:6], S["stt"][:, 5:6], AF.Exp, scale=-0.5),
                          "AC", ("rstd2", t, qc), 1)
                for qc in range(4):
                    R.wait(sc, "DV", ("y", t, qc))
                R.qbarrier(sc, "ST")
                R.inc(sc.dma_start(ybuf[t], S["y_sb"][:]), "ST", ("y_st", t), 16)
                if DEBUG_DUMP == "y":
                    R.inc(sc.dma_start(dbg[t][:, 0:4 * D], S["y_sb"][:]), "ST", ("dbg", t), 16)

            if scalar is not None and DEBUG_DUMP in ("v", "attn", "h"):
                g = scalar
                if DEBUG_DUMP == "h":
                    R.wait(g, "DV", ("h", t))
                    R.inc(g.dma_start(dbg[t][:, 0:5 * D - 64], S["h"][:, 0:5 * D - 64]),
                          "ST", ("dbg", t), 16)
                    R.inc(g.dma_start(dbg[t][:, 5 * D - 40:5 * D].bitcast(F32),
                                      S["st5"][:]), "ST", ("dbgs", t), 16)
                if DEBUG_DUMP == "v":
                    R.wait(g, "DV", ("vev", t, 4, 1))
                    R.inc(g.dma_start(dbg[t], S["vtm"][:]), "ST", ("dbg", t), 16)
                if DEBUG_DUMP == "attn":
                    for qc in range(4):
                        R.wait(g, "DV", ("aev", t, qc))
                        R.wait(g, "PE", ("aTt", t, qc))
                        R.inc(g.dma_start(dbg[t][:, qc * D:(qc + 1) * D], S["a_sb"][:, 0:1024]),
                              "ST", ("dbg", t, qc), 16)

            if tensor is not None:
                te = tensor
                if STAGE < 2:
                    return
                if t == 0:
                    R.wait(te, "IN", ("const", 0))
                    R.wait(te, "DV", ("ones",))
                R.wait(te, "DV", ("h", t))
                for b in range(5):
                    if (t, b) != (0, 0):
                        R.wait(te, "DV", ("hTe", t, b - 1) if b > 0 else ("aTe", t - 1, 3))
                    ins = None
                    for ki in range(NKI):
                        ins = te.transpose(S["psD"][:, ki * 128:(ki + 1) * 128],
                                           S["h"][:, b * D + ki * 128: b * D + (ki + 1) * 128], ident[:])
                    R.inc(ins, "PE", ("hTb", t, b), 1)
                R.alias("PE", ("hT_done", t))
                if STAGE < 3:
                    return
                for b in range(5):
                    R.wait(te, "DV", ("hTe", t, b))
                if t == 0:
                    R.wait(te, "WT", ("attn_wT", 0, NKI - 1))
                for oc in range(NKI):
                    if oc >= 2:
                        R.wait(te, "AC", ("qev", t, oc - 2))
                    elif t > 0:
                        R.wait(te, "DV", ("aev", t - 1, 3))
                    ins = None
                    for ki in range(NKI):
                        ins = te.matmul(S["psA"][:, (oc % 2) * 512:(oc % 2 + 1) * 512],
                                        S["wqT"][:, ki * D + oc * 128: ki * D + (oc + 1) * 128],
                                        S["hT"][:, ki * 640 + 128: ki * 640 + 640],
                                        start=(ki == 0), stop=(ki == NKI - 1))
                    R.inc(ins, "PE", ("qmm", t, oc), 1)
                if t == 0:
                    R.wait(te, "WT", ("attn_wT", 1, NKI - 1))
                for oc in range(NKI):
                    if oc >= 1:
                        R.wait(te, "AC", ("kev", t, oc - 1, 1))
                    elif t > 0:
                        R.wait(te, "DV", ("y", t - 1, 2))
                        R.wait(te, "DV", ("y", t - 1, 3))
                    for part, (c0, cn) in enumerate([(0, 512), (512, 128)]):
                        ins = None
                        for ki in range(NKI):
                            ins = te.matmul(S["psB"][:, c0:c0 + cn],
                                            S["wkT"][:, ki * D + oc * 128: ki * D + (oc + 1) * 128],
                                            S["hT"][:, ki * 640 + c0: ki * 640 + c0 + cn],
                                            start=(ki == 0), stop=(ki == NKI - 1))
                        R.inc(ins, "PE", ("kmm", t, oc, part), 1)
                if t == 0:
                    R.wait(te, "WT", ("attn_wT", 2, NKI - 1))
                for b in range(5):
                    for oh in range(2):
                        if b == 0:
                            R.wait(te, "AC", ("qev", t, 6 + oh))
                        else:
                            R.wait(te, "DV", ("vev", t, b - 1, oh))
                        ins = None
                        for ki in range(NKI):
                            ins = te.matmul(S["psA"][:, oh * 512:(oh + 1) * 512],
                                            S["hT"][:, ki * 640 + b * 128: ki * 640 + (b + 1) * 128],
                                            S["wvT"][:, ki * D + oh * 512: ki * D + (oh + 1) * 512],
                                            start=(ki == 0), stop=(ki == NKI - 1))
                        R.inc(ins, "PE", ("vmm", t, b, oh), 1)
                if STAGE < 4:
                    return
                for qc in range(4):
                    for g4 in range(4):
                        gid = qc * 4 + g4
                        if gid >= 2:
                            pq, pg = ((qc, g4 - 2) if g4 >= 2 else (qc - 1, g4 + 2))
                            R.wait(te, "AC", ("exp", t, pq, pg))
                        elif gid == 0:
                            R.wait(te, "AC", ("kev", t, NKI - 1, 1))
                        psS = S["psB"] if gid % 2 == 0 else S["psC"]
                        ins = None
                        for hh in range(4):
                            hg = g4 * 4 + hh
                            oc = hg // 2
                            qtz = S["qTz"][hg % 2]
                            for jc in range(2):
                                ins = te.matmul(
                                    psS[:, hh * 256 + jc * 128: hh * 256 + (jc + 1) * 128],
                                    S["kT"][:, oc * 640 + (qc + jc) * 128: oc * 640 + (qc + jc + 1) * 128],
                                    qtz[:, oc * TA + qc * 128: oc * TA + (qc + 1) * 128],
                                    start=True, stop=True)
                        R.inc(ins, "PE", ("sT", t, qc, g4), 1)
                    if STAGE < 5:
                        continue
                    if qc == 0:
                        R.wait(te, "DV", ("vev", t, 4, 1))
                    else:
                        R.wait(te, "DV", ("aev", t, qc - 1))
                    for g4 in range(4):
                        R.wait(te, "DV", ("mask", t, qc, g4))
                        pm = S["p_sb"][(qc * 4 + g4) % 2]
                        ins = None
                        for hh in range(4):
                            hg = g4 * 4 + hh
                            for jc in range(2):
                                psl = pm[:, hh * 256 + jc * 128: hh * 256 + (jc + 1) * 128]
                                te.matmul(S["psA"][:, hg * 64:(hg + 1) * 64], psl,
                                          S["vtm"][:, (qc + jc) * D + hg * HD: (qc + jc) * D + (hg + 1) * HD],
                                          start=(jc == 0), stop=(jc == 1))
                                ins = te.matmul(S["psE"][:, 2 * hg:2 * hg + 1], psl, ones_bf[:],
                                                start=(jc == 0), stop=(jc == 1))
                        R.inc(ins, "PE", ("avg", t, qc, g4), 1)
                    R.alias("PE", ("avdone", t, qc))
                    R.wait(te, "DV", ("aev", t, qc))
                    R.wait(te, "DV", ("hTe", t, 4) if qc == 0 else ("aTe", t, qc - 1))
                    ins = None
                    for ki in range(NKI):
                        ins = te.transpose(S["psD"][:, ki * 128:(ki + 1) * 128],
                                           S["a_sb"][:, ki * 128:(ki + 1) * 128], ident[:])
                    R.inc(ins, "PE", ("aTt", t, qc), 1)
                if STAGE < 6:
                    return
                if t == 0:
                    R.wait(te, "WT", ("attn_wT", 3, NKI - 1))
                for qc in range(4):
                    pso = S["psB"] if qc % 2 == 0 else S["psC"]
                    R.wait(te, "DV", ("aTe", t, qc))
                    R.wait(te, "AC", ("exp", t, 3, 2 + (qc % 2)))
                    if qc >= 2 and STAGE >= 7:
                        R.wait(te, "DV", ("y", t, qc - 2))
                    ins = None
                    for oh in range(2):
                        for ki in range(NKI):
                            ins = te.matmul(pso[:, oh * 512:(oh + 1) * 512],
                                            S["aT"][:, ki * TA + qc * 128: ki * TA + (qc + 1) * 128],
                                            S["woT"][:, ki * D + oh * 512: ki * D + (oh + 1) * 512],
                                            start=(ki == 0), stop=(ki == NKI - 1))
                    R.inc(ins, "PE", ("wo", t, qc), 1)

        def emit_phaseB(S, engines):
            sync, scalar, vector, tensor, gpsimd = engines

            if sync is not None:
                s = sync
                R.qbarrier(s, "IN")
                R.qbarrier(s, "WT")
                R.wait(s, "DV", ("y_done", NTA - 1))
                R.wait(s, "ST", ("y_st", NTA - 1))
                R.wait(s, "WC", ("ffn_cast", 0))
                for ki in range(NKI):
                    R.inc(s.dma_start(S["wgT"][:, ki * HID:(ki + 1) * HID],
                                      wbf["wg"][:, ki * 128:(ki + 1) * 128], transpose=True),
                          "WT", ("wgT", ki), 16)
                R.qbarrier(s, "WT")
                R.wait(s, "WC", ("ffn_cast", 1))
                for ki in range(NKI):
                    R.inc(s.dma_start(S["wuT"][:, ki * HID:(ki + 1) * HID],
                                      wbf["wu"][:, ki * 128:(ki + 1) * 128], transpose=True),
                          "WT", ("wuT", ki), 16)
                R.qbarrier(s, "WT")
                R.wait(s, "WC", ("ffn_cast", 2))
                for hc in range(NHC):
                    R.inc(s.dma_start(S["wdT"][:, hc * D:(hc + 1) * D],
                                      wbf["wd"][:, hc * 128:(hc + 1) * 128], transpose=True),
                          "WT", ("wdT", hc), 16)
                for t in range(NTB):
                    R.qbarrier(s, "IN")
                    if t == 0:
                        R.qbarrier(s, "WT")
                    R.wait(s, "ST", ("y_st", min((t * TB + TB - 1) // TA, NTA - 1)))
                    if t >= 2:
                        R.wait(s, "DV", ("outv", t - 2))
                    srcy = ybuf[t // 2][:, (t % 2) * 2 * D:((t % 2) + 1) * 2 * D]
                    R.inc(s.dma_start(S["yt"][t % 2][:], srcy), "IN", ("y", t), 16)

            for t in range(NTB):
                emit_B_tile(S, engines, t)

            if sync is not None:
                R.final_wait(sync, "ST")

        def emit_B_tile(S, engines, t):
            sync, scalar, vector, tensor, gpsimd = engines
            yt = S["yt"][t % 2]

            if vector is not None:
                v = vector
                R.wait(v, "IN", ("y", t))
                if t >= 1:
                    R.wait(v, "PE", ("gT_done", t - 1))
                for b in range(2):
                    v.bn_stats(bn12[:, 0:6], yt[:, b * D:b * D + 512])
                    v.bn_stats(bn12[:, 6:12], yt[:, b * D + 512:(b + 1) * D])
                    v.drain()
                    v.bn_aggr(S["bst"][:, 2 * b:2 * b + 2], bn12[:])
                v.drain()
                R.inc(v.tensor_copy(S["bst"][:, 6:7], S["bst"][:, 0:1]), "DV", ("var3", t), 1)
                R.wait(v, "AC", ("rstd3", t))
                ins = None
                for b in range(2):
                    ins = v.tensor_scalar(S["g"][:, b * D:(b + 1) * D], yt[:, b * D:(b + 1) * D],
                                          S["bst"][:, 2 * b:2 * b + 1], S["bst"][:, 8 + b:9 + b],
                                          ALU.subtract, ALU.mult)
                R.inc(ins, "DV", ("g", t), 1)
                for b in range(2):
                    R.wait(v, "PE", ("gTb", t, b))
                    dst = S["gT"][:].rearrange("p (k n) -> p k n", n=TB)[:, :, b * 128:(b + 1) * 128]
                    R.inc(v.tensor_copy(dst, S["psT"][:].rearrange("p (k n) -> p k n", n=128)),
                          "DV", ("gTe", t, b), 1)
                for hc in range(NHC):
                    R.wait(v, "AC", ("silu", t, hc))
                    R.wait(v, "PE", ("umm", t, hc))
                    psu = S["psU"][hc % 2]
                    R.inc(v.tensor_tensor(S["hh"][:, hc * TB:(hc + 1) * TB],
                                          psu[:, 0:TB], S["sg"][hc % 2][:], ALU.mult),
                          "DV", ("hh", t, hc), 1)
                for tc in range(2):
                    for oh in range(2):
                        R.wait(v, "PE", ("dmm", t, tc, oh))
                        R.inc(v.tensor_copy(S["ffn"][:, tc * D + oh * 512: tc * D + (oh + 1) * 512],
                                            S["psDn"][(tc * 2 + oh) % 2][:]),
                              "DV", ("dev", t, tc, oh), 1)
                if t >= 1:
                    R.wait(v, "ST", ("out_st", t - 1))
                for tc in range(2):
                    v.bn_stats(bn12[:, 0:6], S["ffn"][:, tc * D:tc * D + 512])
                    v.bn_stats(bn12[:, 6:12], S["ffn"][:, tc * D + 512:(tc + 1) * D])
                    v.drain()
                    R.inc(v.bn_aggr(S["bst"][:, 10:12], bn12[:]), "DV", ("var4", t, tc), 1)
                    R.wait(v, "AC", ("rstd4", t, tc))
                    v.tensor_scalar(S["out_sb"][:, tc * D:(tc + 1) * D], S["ffn"][:, tc * D:(tc + 1) * D],
                                    S["bst"][:, 10:11], S["bst"][:, 14:15], ALU.subtract, ALU.mult)
                    ins = v.tensor_tensor(S["out_sb"][:, tc * D:(tc + 1) * D],
                                          S["out_sb"][:, tc * D:(tc + 1) * D],
                                          yt[:, tc * D:(tc + 1) * D], ALU.add)
                    R.inc(ins, "DV", ("outtc", t, tc), 1)
                R.alias("DV", ("outv", t))

            if scalar is not None:
                sc = scalar
                R.wait(sc, "DV", ("var3", t))
                if t >= 1:
                    R.wait(sc, "DV", ("g", t - 1))
                var3_v = S["bst"][:, 0:4].rearrange("p (b k) -> p b k", k=2)[:, :, 1:2]
                rstd3_v = S["bst"][:, 8:10].unsqueeze(2)
                sc.activation(rstd3_v, var3_v, AF.Ln, bias=EPS)
                R.inc(sc.activation(rstd3_v, rstd3_v, AF.Exp, scale=-0.5,
                                    bias=float(np.log(DQ))),
                      "AC", ("rstd3", t), 1)
                for hc in range(NHC):
                    R.wait(sc, "PE", ("gmm", t, hc))
                    if hc >= 2:
                        R.wait(sc, "DV", ("hh", t, hc - 2))
                    elif t >= 1:
                        R.wait(sc, "DV", ("hh", t - 1, NHC - 2 + hc))
                    if hc == 0:
                        sc.activation(waste[:, 0:1], cvals[:, 0:1], AF.Silu)
                    R.inc(sc.activation(S["sg"][hc % 2][:], S["psG"][hc % 2][:, 0:TB], AF.Silu),
                          "AC", ("silu", t, hc), 1)
                for tc in range(2):
                    R.wait(sc, "DV", ("var4", t, tc))
                    if tc == 0 and t >= 1:
                        R.wait(sc, "DV", ("outtc", t - 1, 1))
                    if tc == 0:
                        sc.activation(waste[:, 0:1], S["bst"][:, 11:12], AF.Ln,
                                      bias=EPS * 127.0 * 127.0)
                    sc.activation(S["bst"][:, 14:15], S["bst"][:, 11:12], AF.Ln, bias=EPS * 127.0 * 127.0)
                    R.inc(sc.activation(S["bst"][:, 14:15], S["bst"][:, 14:15], AF.Exp, scale=-0.5),
                          "AC", ("rstd4", t, tc), 1)
                R.wait(sc, "DV", ("outtc", t, 0))
                R.wait(sc, "DV", ("outtc", t, 1))
                R.qbarrier(sc, "ST")
                R.inc(sc.dma_start(out[t], S["out_sb"][:]), "ST", ("out_st", t), 16)

            if tensor is not None:
                te = tensor
                pass
                R.wait(te, "DV", ("g", t))
                for b in range(2):
                    if (t, b) != (0, 0):
                        R.wait(te, "DV", ("gTe", t, b - 1) if b > 0 else ("gTe", t - 1, 1))
                    ins = None
                    for ki in range(NKI):
                        ins = te.transpose(S["psT"][:, ki * 128:(ki + 1) * 128],
                                           S["g"][:, b * D + ki * 128: b * D + (ki + 1) * 128], ident[:])
                    R.inc(ins, "PE", ("gTb", t, b), 1)
                R.alias("PE", ("gT_done", t))
                R.wait(te, "DV", ("gTe", t, 0))
                R.wait(te, "DV", ("gTe", t, 1))
                if t == 0:
                    R.wait(te, "WT", ("wgT", NKI - 1))
                    R.wait(te, "WT", ("wuT", NKI - 1))
                for hc in range(NHC):
                    if hc >= 2:
                        R.wait(te, "AC", ("silu", t, hc - 2))
                        R.wait(te, "DV", ("hh", t, hc - 2))
                    elif t >= 1:
                        R.wait(te, "AC", ("silu", t - 1, NHC - 2 + hc))
                        R.wait(te, "DV", ("hh", t - 1, NHC - 2 + hc))
                    ins = None
                    for ki in range(NKI):
                        ins = te.matmul(S["psG"][hc % 2][:, 0:TB],
                                        S["wgT"][:, ki * HID + hc * 128: ki * HID + (hc + 1) * 128],
                                        S["gT"][:, ki * TB:(ki + 1) * TB],
                                        start=(ki == 0), stop=(ki == NKI - 1))
                    R.inc(ins, "PE", ("gmm", t, hc), 1)
                    ins = None
                    for ki in range(NKI):
                        ins = te.matmul(S["psU"][hc % 2][:, 0:TB],
                                        S["wuT"][:, ki * HID + hc * 128: ki * HID + (hc + 1) * 128],
                                        S["gT"][:, ki * TB:(ki + 1) * TB],
                                        start=(ki == 0), stop=(ki == NKI - 1))
                    R.inc(ins, "PE", ("umm", t, hc), 1)
                if t == 0:
                    R.wait(te, "WT", ("wdT", NHC - 1))
                for tc in range(2):
                    for oh in range(2):
                        did = tc * 2 + oh
                        if (t, did) > (0, 1):
                            pv = (t, did - 2) if did >= 2 else (t - 1, did + 2)
                            R.wait(te, "DV", ("dev", pv[0], pv[1] // 2, pv[1] % 2))
                        ins = None
                        for hc in range(NHC):
                            if tc == 0 and oh == 0:
                                R.wait(te, "DV", ("hh", t, hc))
                            ins = te.matmul(S["psDn"][did % 2][:],
                                            S["hh"][:, hc * TB + tc * 128: hc * TB + (tc + 1) * 128],
                                            S["wdT"][:, hc * D + oh * 512: hc * D + (oh + 1) * 512],
                                            start=(hc == 0), stop=(hc == NHC - 1))
                        R.inc(ins, "PE", ("dmm", t, tc, oh), 1)

        def alloc_A(stack, px=""):
            sba = lambda name, shape, dt=F32: stack.enter_context(nc.sbuf_tensor(px + name, shape, dt))
            S = {}
            S["xt"] = [sba("xt0", [128, 5 * D]), sba("xt1", [128, 5 * D])]
            S["wqT"] = sba("wqT", [128, NKI * D], BF16)
            S["wkT"] = sba("wkT", [128, NKI * D], BF16)
            S["wvT"] = sba("wvT", [128, NKI * D], BF16)
            S["woT"] = sba("woT", [128, NKI * D], BF16)
            S["h"] = sba("h", [128, 5 * D], BF16)
            S["hT"] = sba("hT", [128, NKI * 640], BF16)
            S["qTz"] = [sba("qTz0", [128, NKI * TA], BF16), sba("qTz1", [128, NKI * TA], BF16)]
            S["kT"] = sba("kT", [128, NKI * 640], BF16)
            S["vtm"] = sba("vtm", [128, 5 * D], BF16)
            S["p_raw"] = [sba("p_raw0", [128, 1024], BF16), sba("p_raw1", [128, 1024], BF16)]
            S["p_sb"] = [sba("p_sb0", [128, 1024], BF16), sba("p_sb1", [128, 1024], BF16)]
            S["a_sb"] = sba("a_sb", [128, 1024], BF16)
            S["aT"] = sba("aT", [128, NKI * TA], BF16)
            S["y_sb"] = sba("y_sb", [128, 4 * D])
            S["st5"] = sba("st5", [128, 20])
            S["stt"] = sba("stt", [128, 12])
            S["rs"] = sba("rs", [128, 16])
            S["psE"] = stack.enter_context(nc.psum_tensor(px + "psE", [128, 512], F32))
            S["psA"] = stack.enter_context(nc.psum_tensor(px + "psA", [128, 1024], F32))
            S["psB"] = stack.enter_context(nc.psum_tensor(px + "psB", [128, 1024], F32))
            S["psC"] = stack.enter_context(nc.psum_tensor(px + "psC", [128, 1024], F32))
            S["psD"] = stack.enter_context(nc.psum_tensor(px + "psD", [128, 1024], BF16))
            return S

        def alloc_B(stack, px=""):
            sbb = lambda name, shape, dt=F32: stack.enter_context(nc.sbuf_tensor(px + name, shape, dt))
            S = {}
            S["wgT"] = sbb("wgT", [128, NKI * HID], BF16)
            S["wuT"] = sbb("wuT", [128, NKI * HID], BF16)
            S["wdT"] = sbb("wdT", [128, NHC * D], BF16)
            S["yt"] = [sbb("yt0", [128, 2 * D]), sbb("yt1", [128, 2 * D])]
            S["g"] = sbb("g", [128, 2 * D], BF16)
            S["gT"] = sbb("gT", [128, NKI * TB], BF16)
            S["hh"] = sbb("hh", [128, NHC * TB], BF16)
            S["sg"] = [sbb("sg0", [128, TB]), sbb("sg1", [128, TB])]
            S["ffn"] = sbb("ffn", [128, 2 * D])
            S["out_sb"] = sbb("out_sb", [128, 2 * D])
            S["bst"] = sbb("bst", [128, 16])
            S["psT"] = stack.enter_context(nc.psum_tensor(px + "psT", [128, 1024], BF16))
            S["psG"] = [stack.enter_context(nc.psum_tensor(px + "psG0", [128, 512], F32)),
                        stack.enter_context(nc.psum_tensor(px + "psG1", [128, 512], F32))]
            S["psU"] = [stack.enter_context(nc.psum_tensor(px + "psU0", [128, 512], F32)),
                        stack.enter_context(nc.psum_tensor(px + "psU1", [128, 512], F32))]
            S["psDn"] = [stack.enter_context(nc.psum_tensor(px + "psDn0", [128, 512], F32)),
                         stack.enter_context(nc.psum_tensor(px + "psDn1", [128, 512], F32))]
            return S

        m = MockEng()
        def per_engine(emit, S):
            for idx in range(5):
                engines = [None] * 5
                engines[idx] = m
                emit(S, tuple(engines))

        with contextlib.ExitStack() as p0a:
            SA0 = alloc_A(p0a, px="z0")
            per_engine(emit_phaseA, SA0)
        if "B" in phases:
            with contextlib.ExitStack() as p0b:
                SB0 = alloc_B(p0b, px="z0")
                per_engine(emit_phaseB, SB0)

        sems = {n: es.enter_context(nc.semaphore(n)) for n in semnames}
        R.start_pass1(sems)

        nums = sorted(s.num for s in sems.values())
        clear_range = range(nums[0], nums[-1] + 1)
        with nc.Block() as blk0:
            @blk0.gpsimd
            def _(g):
                g.dma_reset(clear_range)
                g.sem_clear(clear_range)

        blk = es.enter_context(nc.Block())

        pa = contextlib.ExitStack()
        SA = alloc_A(pa)

        pb_holder = {}

        def engine_main(which):
            def f(eng):
                engines = [None] * 5
                idx = {"sync": 0, "scalar": 1, "vector": 2, "tensor": 3, "gpsimd": 4}[which]
                engines[idx] = eng
                sem_of = {"scalar": "AC", "vector": "DV"}
                if which in sem_of:
                    R.eng_for_sem[sem_of[which]] = eng
                emit_phaseA(SA, tuple(engines))
                if "B" in phases:
                    emit_phaseB(pb_holder["SB"], tuple(engines))
                elif which == "sync":
                    R.final_wait(eng, "ST")
                    R.final_wait(eng, "WC")
            return f

        pa.close()
        pb = contextlib.ExitStack()
        SB = alloc_B(pb)
        pb_holder["SB"] = SB

        blk.sync(engine_main("sync"))
        blk.gpsimd(engine_main("gpsimd"))
        blk.vector(engine_main("vector"))
        blk.scalar(engine_main("scalar"))
        blk.tensor(engine_main("tensor"))
        assert R.cnt == R.totals, ("pass mismatch", {k: (R.cnt.get(k), R.totals.get(k))
                                   for k in set(R.cnt) | set(R.totals)
                                   if R.cnt.get(k) != R.totals.get(k)})
        pb.close()

    return nc


def make_masks():
    i = np.arange(128)[:, None]
    j = np.arange(256)[None, :]
    band = (j >= i + 1) & (j <= i + 128)
    first = band & (j >= 128)

    def to_T(m):
        mT = m.T.astype(np.float32)
        outm = np.zeros((128, 256), dtype=np.float32)
        outm[:, 0:128] = mT[0:128, :]
        outm[:, 128:256] = mT[128:256, :]
        return outm.astype(ml_dtypes.bfloat16)
    return to_T(band), to_T(first)


def kernel(L0_attn_x, L0_attn_Wq, L0_attn_Wk, L0_attn_Wv, L0_attn_Wo,
           L0_ffn_w_gate, L0_ffn_w_up, L0_ffn_w_down):
    x = np.ascontiguousarray(np.asarray(L0_attn_x, dtype=np.float32)).reshape(32768, D)
    mask_stdT, mask_firstT = make_masks()
    ident = np.eye(128, dtype=np.float32).astype(ml_dtypes.bfloat16)

    nc = build_program(DEBUG_DUMP, PHASES)
    in_maps = []
    for c in range(8):
        r0 = c * TOK
        halo = np.zeros((128, D), dtype=np.float32) if c % 4 == 0 else x[r0 - 128:r0]
        xhc = np.concatenate([halo, x[r0:r0 + TOK]], axis=0)
        xt_t = np.stack([xhc[t * TA:t * TA + TA + 128].reshape(5, 128, D)
                         .transpose(1, 0, 2).reshape(128, 5 * D) for t in range(NTA)])
        in_maps.append(dict(
            xh=np.ascontiguousarray(xt_t),
            wq=np.ascontiguousarray(np.asarray(L0_attn_Wq, dtype=np.float32)),
            wk=np.ascontiguousarray(np.asarray(L0_attn_Wk, dtype=np.float32)),
            wv=np.ascontiguousarray(np.asarray(L0_attn_Wv, dtype=np.float32)),
            wo=np.ascontiguousarray(np.asarray(L0_attn_Wo, dtype=np.float32)),
            wg=np.ascontiguousarray(np.asarray(L0_ffn_w_gate, dtype=np.int32)),
            wu=np.ascontiguousarray(np.asarray(L0_ffn_w_up, dtype=np.int32)),
            wd=np.ascontiguousarray(np.asarray(L0_ffn_w_down, dtype=np.int32)),
            ident_in=ident,
            consts_in=np.concatenate([np.tile(np.array([[EPS, EPS * 127.0 * 127.0, float(np.log(DQ)), 0.0]],
                                                   dtype=np.float32), (128, 1)),
                              np.where(np.arange(128)[:, None] < 64, QSCALE, 0.0).astype(np.float32),
                              np.where(np.arange(128)[:, None] >= 64, QSCALE, 0.0).astype(np.float32),
                              np.zeros((128, 2), np.float32)], axis=1),
            mask_first_in=mask_firstT if c % 4 == 0 else mask_stdT,
            mask_std_in=mask_stdT,
        ))
    res = run_bass_kernel_spmd(nc, in_maps, core_ids=list(range(8)))
    parts = []
    for r in res.results:
        o = np.asarray(r["out"])
        o = o.reshape(NTB, 128, 2, D).transpose(0, 2, 1, 3).reshape(TOK, D)
        parts.append(o)
    return np.concatenate(parts, axis=0).reshape(2, 16384, D)


# version 04
# speedup vs baseline: 1.0107x; 1.0107x over previous
import contextlib
import numpy as np
import ml_dtypes
import concourse.bass as bass
import concourse.mybir as mybir
from concourse.bass_utils import run_bass_kernel_spmd

F32 = mybir.dt.float32
BF16 = mybir.dt.bfloat16
I32 = mybir.dt.int32
AF = mybir.ActivationFunctionType
ALU = mybir.AluOpType
AX = mybir.AxisListType

D = 1024
NH = 16
HD = 64
TOK = 4096
TA = 512
NTA = TOK // TA
TB = 256
NTB = TOK // TB
HID = 3072
NKI = D // 128
NHC = HID // 128
EPS = 1e-5
QSCALE = 1.0 / 8.0
DQ = 1.0 / 127.0

DEBUG_DUMP = None
PHASES = "AB"
STAGE = 7


class MockInst:
    def then_inc(self, *a, **k):
        return self


class MockEng:
    def __getattr__(self, name):
        def f(*a, **k):
            return MockInst()
        return f


DMA_SEMS = {"IN", "WC", "WT", "WU", "ST"}


class Reg:

    def __init__(self):
        self.phase = 0
        self.cnt = {}
        self.miles = {}
        self.pending = {}
        self.totals = None
        self.sems = None
        self.eng_for_sem = {}

    def start_pass1(self, sems):
        for sem, keys in self.pending.items():
            for key in keys:
                self.miles[(sem, key)] = self.cnt.get(sem, 0)
        self.pending = {}
        self.phase = 1
        self.sems = sems
        self.totals = dict(self.cnt)
        self.cnt = {}

    def inc(self, inst, sem, key=None, amt=1):
        self.cnt[sem] = self.cnt.get(sem, 0) + amt
        if self.phase == 0:
            if key is not None:
                assert (sem, key) not in self.miles and \
                    key not in self.pending.get(sem, []), (sem, key)
                if sem in DMA_SEMS:
                    self.pending.setdefault(sem, []).append(key)
                else:
                    self.miles[(sem, key)] = self.cnt[sem]
        else:
            eng = self.eng_for_sem.get(sem)
            if eng is not None:
                eng.drain().then_inc(self.sems[sem], amt)
            else:
                inst.then_inc(self.sems[sem], amt)
        return inst

    def alias(self, sem, key):
        if self.phase == 0:
            assert (sem, key) not in self.miles, (sem, key)
            self.miles[(sem, key)] = self.cnt.get(sem, 0)

    def stable(self, *sems_):
        if self.phase != 0:
            return
        for sem in sems_:
            for key in self.pending.pop(sem, []):
                self.miles[(sem, key)] = self.cnt.get(sem, 0)

    def qbarrier(self, eng, sem):
        if self.phase == 0:
            self.stable(sem)
        else:
            eng.wait_ge(self.sems[sem], self.cnt.get(sem, 0))

    def wait(self, eng, sem, key):
        if self.phase == 0:
            return
        val = self.miles[(sem, key)]
        assert val <= self.totals.get(sem, 0), ("wait exceeds total", sem, key, val, self.totals.get(sem, 0))
        eng.wait_ge(self.sems[sem], val)

    def final_wait(self, eng, sem):
        if self.phase == 0:
            return
        tot = self.totals.get(sem, 0)
        if tot:
            eng.wait_ge(self.sems[sem], tot)


def bcast(ap, shape):
    return ap.broadcast_to(shape)


def build_program(debug_dump=None, phases="AB"):
    nc = bass.Bass("TRN2", target_bir_lowering=False, debug=False,
                   detect_race_conditions=False)

    xh = nc.dram_tensor("xh", [NTA, 128, 5 * D], F32, kind="ExternalInput").ap()
    wq = nc.dram_tensor("wq", [D, D], F32, kind="ExternalInput").ap()
    wk = nc.dram_tensor("wk", [D, D], F32, kind="ExternalInput").ap()
    wv = nc.dram_tensor("wv", [D, D], F32, kind="ExternalInput").ap()
    wo = nc.dram_tensor("wo", [D, D], F32, kind="ExternalInput").ap()
    wg = nc.dram_tensor("wg", [HID, D], I32, kind="ExternalInput").ap()
    wu = nc.dram_tensor("wu", [HID, D], I32, kind="ExternalInput").ap()
    wd = nc.dram_tensor("wd", [D, HID], I32, kind="ExternalInput").ap()
    ident_in = nc.dram_tensor("ident_in", [128, 128], BF16, kind="ExternalInput").ap()
    consts_in = nc.dram_tensor("consts_in", [128, 8], F32, kind="ExternalInput").ap()
    mask_first_in = nc.dram_tensor("mask_first_in", [128, 256], BF16, kind="ExternalInput").ap()
    mask_std_in = nc.dram_tensor("mask_std_in", [128, 256], BF16, kind="ExternalInput").ap()

    out = nc.dram_tensor("out", [NTB, 128, 2 * D], F32, kind="ExternalOutput").ap()
    dbg = None
    if debug_dump:
        ddt = F32 if debug_dump == "y" else BF16
        dbg = nc.dram_tensor("dbg", [NTA, 128, 5 * D], ddt, kind="ExternalOutput").ap()

    ybuf = nc.dram_tensor("ybuf", [NTA, 128, 4 * D], F32, kind="Internal").ap()
    wbf = {n: nc.dram_tensor(n + "_bf", list(shape), BF16, kind="Internal").ap()
           for n, shape in [("wq", (D, D)), ("wk", (D, D)), ("wv", (D, D)), ("wo", (D, D)),
                            ("wg", (HID, D)), ("wu", (HID, D)), ("wd", (D, HID))]}

    R = Reg()

    es = contextlib.ExitStack()
    with es:
        sbp = lambda name, shape, dt=F32: es.enter_context(nc.sbuf_tensor(name, shape, dt))
        ident = sbp("ident", [128, 128], BF16)
        mask_first = sbp("mask_first", [128, 256], BF16)
        mask_std = sbp("mask_std", [128, 256], BF16)
        ones_bf = sbp("ones_bf", [128, 1], BF16)
        waste = sbp("waste", [128, 8], F32)
        bn12 = sbp("bn12", [128, 12], F32)
        cvals = sbp("cvals", [128, 8], F32)
        nc.const_aps.register(F32, EPS, cvals[:, 0:1])
        nc.const_aps.register(F32, EPS * 127.0 * 127.0, cvals[:, 1:2])
        nc.const_aps.register(F32, float(np.log(DQ)), cvals[:, 2:3])
        semnames = ["IN", "WC", "WT", "WU", "PE", "DV", "AC", "ST"]

        def emit_phaseA(S, engines):
            sync, scalar, vector, tensor, gpsimd = engines

            if gpsimd is not None:
                g = gpsimd
                for i, (src, dst) in enumerate([(wq, wbf["wq"]), (wk, wbf["wk"]),
                                                (wv, wbf["wv"]), (wo, wbf["wo"])]):
                    R.inc(g.dma_start(dst, src), "WC", ("attn_cast", i), 16)
                    R.qbarrier(g, "WC")
                for i, (src, dst) in enumerate([(wg, wbf["wg"]), (wu, wbf["wu"]), (wd, wbf["wd"])]):
                    R.inc(g.dma_start(dst, src), "WC", ("ffn_cast", i), 16)
                    R.qbarrier(g, "WC")

            if sync is not None:
                s = sync
                R.inc(s.dma_start(ident[:], ident_in), "IN", ("const", 0), 16)
                R.inc(s.dma_start(mask_first[:], mask_first_in), "IN", ("const", 1), 16)
                R.inc(s.dma_start(mask_std[:], mask_std_in), "IN", ("const", 2), 16)
                R.inc(s.dma_start(cvals[:], consts_in), "IN", ("const", 3), 16)
                for i, (name, dstT) in enumerate([("wq", S["wqT"]), ("wk", S["wkT"]),
                                                  ("wv", S["wvT"]), ("wo", S["woT"])]):
                    R.qbarrier(s, "IN")
                    R.qbarrier(s, "WT")
                    R.wait(s, "WC", ("attn_cast", i))
                    for ki in range(NKI):
                        R.inc(s.dma_start(dstT[:, ki * D:(ki + 1) * D],
                                          wbf[name][:, ki * 128:(ki + 1) * 128], transpose=True),
                              "WT", ("attn_wT", i, ki), 16)
                for t in range(NTA):
                    if t >= 2:
                        R.qbarrier(s, "IN")
                        if STAGE >= 7:
                            R.wait(s, "DV", ("y_done", t - 2))
                    R.inc(s.dma_start(S["xt"][t % 2][:], xh[t]), "IN", ("x", t), 16)

            for t in range(NTA):
                emit_A_tile(S, engines, t)

        def emit_A_tile(S, engines, t):
            sync, scalar, vector, tensor, gpsimd = engines
            xt = S["xt"][t % 2]
            first_chunk_tile = (t == 0)

            if vector is not None:
                v = vector
                if t == 0:
                    R.wait(v, "IN", ("const", 2))
                    v.memset(S["psE"][:, 0:32], 1.0)
                    R.inc(v.memset(ones_bf[:], 1.0), "DV", ("ones",), 1)
                R.wait(v, "IN", ("x", t))
                if t >= 1:
                    R.wait(v, "PE", ("hT_done", t - 1))
                for b in range(5):
                    v.bn_stats(bn12[:, 0:6], xt[:, b * D:b * D + 512])
                    v.bn_stats(bn12[:, 6:12], xt[:, b * D + 512:(b + 1) * D])
                    v.drain()
                    v.bn_aggr(S["st5"][:, 2 * b:2 * b + 2], bn12[:])
                v.drain()
                R.inc(v.tensor_copy(S["stt"][:, 0:1], S["st5"][:, 0:1]), "DV", ("var1", t), 1)
                R.wait(v, "AC", ("rstd1", t))
                ins = None
                for b in range(5):
                    ins = v.tensor_scalar(S["h"][:, b * D:(b + 1) * D], xt[:, b * D:(b + 1) * D],
                                          S["st5"][:, 2 * b:2 * b + 1], S["st5"][:, 10 + b:11 + b],
                                          ALU.subtract, ALU.mult)
                R.inc(ins, "DV", ("h", t), 1)
                if STAGE < 2:
                    return
                for b in range(5):
                    R.wait(v, "PE", ("hTb", t, b))
                    dst = S["hT"][:].rearrange("p (k n) -> p k n", n=640)[:, :, b * 128:(b + 1) * 128]
                    R.inc(v.tensor_copy(dst, S["psD"][:].rearrange("p (k n) -> p k n", n=128)),
                          "DV", ("hTe", t, b), 1)
                if STAGE < 3:
                    return
                for b in range(5):
                    for oh in range(2):
                        R.wait(v, "PE", ("vmm", t, b, oh))
                        R.inc(v.tensor_copy(S["vtm"][:, b * D + oh * 512: b * D + (oh + 1) * 512],
                                            S["psA"][:, oh * 512:(oh + 1) * 512]),
                              "DV", ("vev", t, b, oh), 1)
                if STAGE < 4:
                    return
                for qc in range(4):
                    for g4 in range(4):
                        gid = qc * 4 + g4
                        R.wait(v, "AC", ("exp", t, qc, g4))
                        if (t, gid) > (0, 1) and STAGE >= 5:
                            pq, pg = ((qc, g4 - 2) if g4 >= 2 else (qc - 1, g4 + 2))
                            if pq >= 0:
                                R.wait(v, "PE", ("avg", t, pq, pg))
                            elif t > 0:
                                R.wait(v, "PE", ("avg", t - 1, 3, pg))
                        mk = mask_first if (first_chunk_tile and qc == 0) else mask_std
                        mk4 = mk[:].rearrange("p (j q) -> p j q", j=2).unsqueeze(1)
                        pr = S["p_raw"][gid % 2]
                        pm = S["p_sb"][gid % 2]
                        R.inc(v.tensor_tensor(pm[:].rearrange("p (g j q) -> p g j q", g=4, j=2),
                                              pr[:].rearrange("p (g j q) -> p g j q", g=4, j=2),
                                              bcast(mk4, [128, 4, 2, 128]), ALU.mult),
                              "DV", ("mask", t, qc, g4), 1)
                    if STAGE < 5:
                        continue
                    R.wait(v, "PE", ("avdone", t, qc))
                    psE_v = S["psE"][:, 0:32].rearrange("p (h k) -> p h k", k=2)[:, :, 0:1]
                    v.reciprocal(S["rs"][:, 0:16].unsqueeze(2), psE_v)
                    v.drain()
                    ins = None
                    for bk in range(2):
                        rs3 = bcast(S["rs"][:, bk * 8:(bk + 1) * 8].unsqueeze(2), [128, 8, 64])
                        ins = v.tensor_tensor(
                            S["a_sb"][:, bk * 512:(bk + 1) * 512].rearrange("p (h d) -> p h d", d=64),
                            S["psA"][:, bk * 512:(bk + 1) * 512].rearrange("p (h d) -> p h d", d=64),
                            rs3, ALU.mult)
                    R.inc(ins, "DV", ("aev", t, qc), 1)
                    R.wait(v, "PE", ("aTt", t, qc))
                    dst = S["aT"][:].rearrange("p (k n) -> p k n", n=TA)[:, :, qc * 128:(qc + 1) * 128]
                    R.inc(v.tensor_copy(dst, S["psD"][:].rearrange("p (k n) -> p k n", n=128)),
                          "DV", ("aTe", t, qc), 1)
                if STAGE < 7:
                    return
                if t >= 1:
                    R.wait(v, "ST", ("y_st", t - 1))
                for qc in range(4):
                    pso = S["psB"] if qc % 2 == 0 else S["psC"]
                    R.wait(v, "PE", ("wo", t, qc))
                    v.bn_stats(bn12[:, 0:6], pso[:, 0:512])
                    v.bn_stats(bn12[:, 6:12], pso[:, 512:1024])
                    v.drain()
                    R.inc(v.bn_aggr(S["stt"][:, 1:3], bn12[:]), "DV", ("var2", t, qc), 1)
                    R.wait(v, "AC", ("rstd2", t, qc))
                    v.tensor_scalar(S["y_sb"][:, qc * D:(qc + 1) * D], pso[:],
                                    S["stt"][:, 1:2], S["stt"][:, 5:6], ALU.subtract, ALU.mult)
                    ins = v.tensor_tensor(S["y_sb"][:, qc * D:(qc + 1) * D],
                                          S["y_sb"][:, qc * D:(qc + 1) * D],
                                          xt[:, (1 + qc) * D:(2 + qc) * D], ALU.add)
                    R.inc(ins, "DV", ("y", t, qc), 1)
                R.alias("DV", ("y_done", t))

            if scalar is not None:
                sc = scalar
                if t == 0:
                    R.wait(sc, "IN", ("const", 3))
                    sc.activation(waste[:], cvals[:, 0:8], AF.Square)
                    sc.drain()
                R.wait(sc, "DV", ("var1", t))
                if t >= 1:
                    R.wait(sc, "DV", ("h", t - 1))
                var_v = S["st5"][:, 0:10].rearrange("p (b k) -> p b k", k=2)[:, :, 1:2]
                rstd_v = S["st5"][:, 10:15].unsqueeze(2)
                sc.activation(rstd_v, var_v, AF.Ln, bias=EPS)
                R.inc(sc.activation(rstd_v, rstd_v, AF.Exp, scale=-0.5),
                      "AC", ("rstd1", t), 1)
                if STAGE < 3:
                    return
                for oc in range(NKI):
                    R.wait(sc, "PE", ("qmm", t, oc))
                    sc.activation(S["qTz"][0][:, oc * TA:(oc + 1) * TA],
                                  S["psA"][:, (oc % 2) * 512:(oc % 2 + 1) * 512],
                                  AF.Copy, scale=cvals[:, 4:5])
                    R.inc(sc.activation(S["qTz"][1][:, oc * TA:(oc + 1) * TA],
                                        S["psA"][:, (oc % 2) * 512:(oc % 2 + 1) * 512],
                                        AF.Copy, scale=cvals[:, 5:6]),
                          "AC", ("qev", t, oc), 1)
                for oc in range(NKI):
                    for part, (c0, cn) in enumerate([(0, 512), (512, 128)]):
                        R.wait(sc, "PE", ("kmm", t, oc, part))
                        R.inc(sc.activation(S["kT"][:, oc * 640 + c0: oc * 640 + c0 + cn],
                                            S["psB"][:, c0:c0 + cn], AF.Copy),
                              "AC", ("kev", t, oc, part), 1)
                if STAGE < 4:
                    return
                for qc in range(4):
                    for g4 in range(4):
                        gid = qc * 4 + g4
                        R.wait(sc, "PE", ("sT", t, qc, g4))
                        if (t, gid) > (0, 1):
                            pq, pg = ((qc, g4 - 2) if g4 >= 2 else (qc - 1, g4 + 2))
                            if pq >= 0:
                                R.wait(sc, "DV", ("mask", t, pq, pg))
                            elif t > 0:
                                R.wait(sc, "DV", ("mask", t - 1, 3, pg))
                        psS = S["psB"] if gid % 2 == 0 else S["psC"]
                        pr = S["p_raw"][gid % 2]
                        sc.activation(pr[:, 0:512], psS[:, 0:512], AF.Exp)
                        R.inc(sc.activation(pr[:, 512:1024], psS[:, 512:1024], AF.Exp),
                              "AC", ("exp", t, qc, g4), 1)
                if STAGE < 7:
                    return
                for qc in range(4):
                    R.wait(sc, "DV", ("var2", t, qc))
                    if qc == 0 and t >= 1:
                        R.wait(sc, "DV", ("y", t - 1, 3))
                    sc.activation(S["stt"][:, 5:6], S["stt"][:, 2:3], AF.Ln, bias=EPS)
                    R.inc(sc.activation(S["stt"][:, 5:6], S["stt"][:, 5:6], AF.Exp, scale=-0.5),
                          "AC", ("rstd2", t, qc), 1)
                for qc in range(4):
                    R.wait(sc, "DV", ("y", t, qc))
                R.qbarrier(sc, "ST")
                R.inc(sc.dma_start(ybuf[t], S["y_sb"][:]), "ST", ("y_st", t), 16)
                if DEBUG_DUMP == "y":
                    R.inc(sc.dma_start(dbg[t][:, 0:4 * D], S["y_sb"][:]), "ST", ("dbg", t), 16)

            if scalar is not None and DEBUG_DUMP in ("v", "attn", "h"):
                g = scalar
                if DEBUG_DUMP == "h":
                    R.wait(g, "DV", ("h", t))
                    R.inc(g.dma_start(dbg[t][:, 0:5 * D - 64], S["h"][:, 0:5 * D - 64]),
                          "ST", ("dbg", t), 16)
                    R.inc(g.dma_start(dbg[t][:, 5 * D - 40:5 * D].bitcast(F32),
                                      S["st5"][:]), "ST", ("dbgs", t), 16)
                if DEBUG_DUMP == "v":
                    R.wait(g, "DV", ("vev", t, 4, 1))
                    R.inc(g.dma_start(dbg[t], S["vtm"][:]), "ST", ("dbg", t), 16)
                if DEBUG_DUMP == "attn":
                    for qc in range(4):
                        R.wait(g, "DV", ("aev", t, qc))
                        R.wait(g, "PE", ("aTt", t, qc))
                        R.inc(g.dma_start(dbg[t][:, qc * D:(qc + 1) * D], S["a_sb"][:, 0:1024]),
                              "ST", ("dbg", t, qc), 16)

            if tensor is not None:
                te = tensor
                if STAGE < 2:
                    return
                if t == 0:
                    R.wait(te, "IN", ("const", 0))
                    R.wait(te, "DV", ("ones",))
                R.wait(te, "DV", ("h", t))
                for b in range(5):
                    if (t, b) != (0, 0):
                        R.wait(te, "DV", ("hTe", t, b - 1) if b > 0 else ("aTe", t - 1, 3))
                    ins = None
                    for ki in range(NKI):
                        ins = te.transpose(S["psD"][:, ki * 128:(ki + 1) * 128],
                                           S["h"][:, b * D + ki * 128: b * D + (ki + 1) * 128], ident[:])
                    R.inc(ins, "PE", ("hTb", t, b), 1)
                R.alias("PE", ("hT_done", t))
                if STAGE < 3:
                    return
                for b in range(5):
                    R.wait(te, "DV", ("hTe", t, b))
                if t == 0:
                    R.wait(te, "WT", ("attn_wT", 0, NKI - 1))
                for oc in range(NKI):
                    if oc >= 2:
                        R.wait(te, "AC", ("qev", t, oc - 2))
                    elif t > 0:
                        R.wait(te, "DV", ("aev", t - 1, 3))
                    ins = None
                    for ki in range(NKI):
                        ins = te.matmul(S["psA"][:, (oc % 2) * 512:(oc % 2 + 1) * 512],
                                        S["wqT"][:, ki * D + oc * 128: ki * D + (oc + 1) * 128],
                                        S["hT"][:, ki * 640 + 128: ki * 640 + 640],
                                        start=(ki == 0), stop=(ki == NKI - 1))
                    R.inc(ins, "PE", ("qmm", t, oc), 1)
                if t == 0:
                    R.wait(te, "WT", ("attn_wT", 1, NKI - 1))
                for oc in range(NKI):
                    if oc >= 1:
                        R.wait(te, "AC", ("kev", t, oc - 1, 1))
                    elif t > 0:
                        R.wait(te, "DV", ("y", t - 1, 2))
                        R.wait(te, "DV", ("y", t - 1, 3))
                    for part, (c0, cn) in enumerate([(0, 512), (512, 128)]):
                        ins = None
                        for ki in range(NKI):
                            ins = te.matmul(S["psB"][:, c0:c0 + cn],
                                            S["wkT"][:, ki * D + oc * 128: ki * D + (oc + 1) * 128],
                                            S["hT"][:, ki * 640 + c0: ki * 640 + c0 + cn],
                                            start=(ki == 0), stop=(ki == NKI - 1))
                        R.inc(ins, "PE", ("kmm", t, oc, part), 1)
                if t == 0:
                    R.wait(te, "WT", ("attn_wT", 2, NKI - 1))
                for b in range(5):
                    for oh in range(2):
                        if b == 0:
                            R.wait(te, "AC", ("qev", t, 6 + oh))
                        else:
                            R.wait(te, "DV", ("vev", t, b - 1, oh))
                        ins = None
                        for ki in range(NKI):
                            ins = te.matmul(S["psA"][:, oh * 512:(oh + 1) * 512],
                                            S["hT"][:, ki * 640 + b * 128: ki * 640 + (b + 1) * 128],
                                            S["wvT"][:, ki * D + oh * 512: ki * D + (oh + 1) * 512],
                                            start=(ki == 0), stop=(ki == NKI - 1))
                        R.inc(ins, "PE", ("vmm", t, b, oh), 1)
                if STAGE < 4:
                    return
                for qc in range(4):
                    for g4 in range(4):
                        gid = qc * 4 + g4
                        if gid >= 2:
                            pq, pg = ((qc, g4 - 2) if g4 >= 2 else (qc - 1, g4 + 2))
                            R.wait(te, "AC", ("exp", t, pq, pg))
                        elif gid == 0:
                            R.wait(te, "AC", ("kev", t, NKI - 1, 1))
                        psS = S["psB"] if gid % 2 == 0 else S["psC"]
                        ins = None
                        for hh in range(4):
                            hg = g4 * 4 + hh
                            oc = hg // 2
                            qtz = S["qTz"][hg % 2]
                            for jc in range(2):
                                ins = te.matmul(
                                    psS[:, hh * 256 + jc * 128: hh * 256 + (jc + 1) * 128],
                                    S["kT"][:, oc * 640 + (qc + jc) * 128: oc * 640 + (qc + jc + 1) * 128],
                                    qtz[:, oc * TA + qc * 128: oc * TA + (qc + 1) * 128],
                                    start=True, stop=True)
                        R.inc(ins, "PE", ("sT", t, qc, g4), 1)
                    if STAGE < 5:
                        continue
                    if qc == 0:
                        R.wait(te, "DV", ("vev", t, 4, 1))
                    else:
                        R.wait(te, "DV", ("aev", t, qc - 1))
                    for g4 in range(4):
                        R.wait(te, "DV", ("mask", t, qc, g4))
                        pm = S["p_sb"][(qc * 4 + g4) % 2]
                        ins = None
                        for hh in range(4):
                            hg = g4 * 4 + hh
                            for jc in range(2):
                                psl = pm[:, hh * 256 + jc * 128: hh * 256 + (jc + 1) * 128]
                                te.matmul(S["psA"][:, hg * 64:(hg + 1) * 64], psl,
                                          S["vtm"][:, (qc + jc) * D + hg * HD: (qc + jc) * D + (hg + 1) * HD],
                                          start=(jc == 0), stop=(jc == 1))
                                ins = te.matmul(S["psE"][:, 2 * hg:2 * hg + 1], psl, ones_bf[:],
                                                start=(jc == 0), stop=(jc == 1))
                        R.inc(ins, "PE", ("avg", t, qc, g4), 1)
                    R.alias("PE", ("avdone", t, qc))
                    R.wait(te, "DV", ("aev", t, qc))
                    R.wait(te, "DV", ("hTe", t, 4) if qc == 0 else ("aTe", t, qc - 1))
                    ins = None
                    for ki in range(NKI):
                        ins = te.transpose(S["psD"][:, ki * 128:(ki + 1) * 128],
                                           S["a_sb"][:, ki * 128:(ki + 1) * 128], ident[:])
                    R.inc(ins, "PE", ("aTt", t, qc), 1)
                if STAGE < 6:
                    return
                if t == 0:
                    R.wait(te, "WT", ("attn_wT", 3, NKI - 1))
                for qc in range(4):
                    pso = S["psB"] if qc % 2 == 0 else S["psC"]
                    R.wait(te, "DV", ("aTe", t, qc))
                    R.wait(te, "AC", ("exp", t, 3, 2 + (qc % 2)))
                    if qc >= 2 and STAGE >= 7:
                        R.wait(te, "DV", ("y", t, qc - 2))
                    ins = None
                    for oh in range(2):
                        for ki in range(NKI):
                            ins = te.matmul(pso[:, oh * 512:(oh + 1) * 512],
                                            S["aT"][:, ki * TA + qc * 128: ki * TA + (qc + 1) * 128],
                                            S["woT"][:, ki * D + oh * 512: ki * D + (oh + 1) * 512],
                                            start=(ki == 0), stop=(ki == NKI - 1))
                    R.inc(ins, "PE", ("wo", t, qc), 1)

        def emit_phaseB(S, engines):
            sync, scalar, vector, tensor, gpsimd = engines

            if sync is not None:
                s = sync
                R.qbarrier(s, "IN")
                R.qbarrier(s, "WT")
                R.wait(s, "DV", ("y_done", NTA - 1))
                R.wait(s, "ST", ("y_st", NTA - 1))
                R.wait(s, "WC", ("ffn_cast", 0))
                for ki in range(NKI):
                    R.inc(s.dma_start(S["wgT"][:, ki * HID:(ki + 1) * HID],
                                      wbf["wg"][:, ki * 128:(ki + 1) * 128], transpose=True),
                          "WT", ("wgT", ki), 16)
                R.qbarrier(s, "WT")
                R.wait(s, "WC", ("ffn_cast", 1))
                for ki in range(NKI):
                    R.inc(s.dma_start(S["wuT"][:, ki * HID:(ki + 1) * HID],
                                      wbf["wu"][:, ki * 128:(ki + 1) * 128], transpose=True),
                          "WT", ("wuT", ki), 16)
                R.qbarrier(s, "WT")
                R.wait(s, "WC", ("ffn_cast", 2))
                for hc in range(NHC):
                    R.inc(s.dma_start(S["wdT"][:, hc * D:(hc + 1) * D],
                                      wbf["wd"][:, hc * 128:(hc + 1) * 128], transpose=True),
                          "WT", ("wdT", hc), 16)
                for t in range(NTB):
                    R.qbarrier(s, "IN")
                    if t == 0:
                        R.qbarrier(s, "WT")
                    R.wait(s, "ST", ("y_st", min((t * TB + TB - 1) // TA, NTA - 1)))
                    if t >= 2:
                        R.wait(s, "DV", ("outv", t - 2))
                    srcy = ybuf[t // 2][:, (t % 2) * 2 * D:((t % 2) + 1) * 2 * D]
                    R.inc(s.dma_start(S["yt"][t % 2][:], srcy), "IN", ("y", t), 16)

            for t in range(NTB):
                emit_B_tile(S, engines, t)

            if sync is not None:
                R.final_wait(sync, "ST")

        def emit_B_tile(S, engines, t):
            sync, scalar, vector, tensor, gpsimd = engines
            yt = S["yt"][t % 2]

            if vector is not None:
                v = vector
                R.wait(v, "IN", ("y", t))
                if t >= 1:
                    R.wait(v, "PE", ("gT_done", t - 1))
                for b in range(2):
                    v.bn_stats(bn12[:, 0:6], yt[:, b * D:b * D + 512])
                    v.bn_stats(bn12[:, 6:12], yt[:, b * D + 512:(b + 1) * D])
                    v.drain()
                    v.bn_aggr(S["bst"][:, 2 * b:2 * b + 2], bn12[:])
                v.drain()
                R.inc(v.tensor_copy(S["bst"][:, 6:7], S["bst"][:, 0:1]), "DV", ("var3", t), 1)
                R.wait(v, "AC", ("rstd3", t))
                ins = None
                for b in range(2):
                    ins = v.tensor_scalar(S["g"][:, b * D:(b + 1) * D], yt[:, b * D:(b + 1) * D],
                                          S["bst"][:, 2 * b:2 * b + 1], S["bst"][:, 8 + b:9 + b],
                                          ALU.subtract, ALU.mult)
                R.inc(ins, "DV", ("g", t), 1)
                for b in range(2):
                    R.wait(v, "PE", ("gTb", t, b))
                    dst = S["gT"][:].rearrange("p (k n) -> p k n", n=TB)[:, :, b * 128:(b + 1) * 128]
                    R.inc(v.tensor_copy(dst, S["psT"][:].rearrange("p (k n) -> p k n", n=128)),
                          "DV", ("gTe", t, b), 1)
                for p in range(NHC // 2):
                    R.wait(v, "AC", ("silu", t, p))
                    R.wait(v, "PE", ("umm", t, 2 * p + 1))
                    psu = S["psU"][p % 2]
                    R.inc(v.tensor_tensor(S["hh"][:, 2 * p * TB:(2 * p + 2) * TB],
                                          psu[:], S["sg"][p % 2][:], ALU.mult),
                          "DV", ("hh", t, p), 1)
                for tc in range(2):
                    for oh in range(2):
                        R.wait(v, "PE", ("dmm", t, tc, oh))
                        R.inc(v.tensor_copy(S["ffn"][:, tc * D + oh * 512: tc * D + (oh + 1) * 512],
                                            S["psDn"][(tc * 2 + oh) % 2][:]),
                              "DV", ("dev", t, tc, oh), 1)
                if t >= 1:
                    R.wait(v, "ST", ("out_st", t - 1))
                for tc in range(2):
                    v.bn_stats(bn12[:, 0:6], S["ffn"][:, tc * D:tc * D + 512])
                    v.bn_stats(bn12[:, 6:12], S["ffn"][:, tc * D + 512:(tc + 1) * D])
                    v.drain()
                    R.inc(v.bn_aggr(S["bst"][:, 10:12], bn12[:]), "DV", ("var4", t, tc), 1)
                    R.wait(v, "AC", ("rstd4", t, tc))
                    v.tensor_scalar(S["out_sb"][:, tc * D:(tc + 1) * D], S["ffn"][:, tc * D:(tc + 1) * D],
                                    S["bst"][:, 10:11], S["bst"][:, 14:15], ALU.subtract, ALU.mult)
                    ins = v.tensor_tensor(S["out_sb"][:, tc * D:(tc + 1) * D],
                                          S["out_sb"][:, tc * D:(tc + 1) * D],
                                          yt[:, tc * D:(tc + 1) * D], ALU.add)
                    R.inc(ins, "DV", ("outtc", t, tc), 1)
                R.alias("DV", ("outv", t))

            if scalar is not None:
                sc = scalar
                R.wait(sc, "DV", ("var3", t))
                if t >= 1:
                    R.wait(sc, "DV", ("g", t - 1))
                var3_v = S["bst"][:, 0:4].rearrange("p (b k) -> p b k", k=2)[:, :, 1:2]
                rstd3_v = S["bst"][:, 8:10].unsqueeze(2)
                sc.activation(rstd3_v, var3_v, AF.Ln, bias=EPS)
                R.inc(sc.activation(rstd3_v, rstd3_v, AF.Exp, scale=-0.5,
                                    bias=float(np.log(DQ))),
                      "AC", ("rstd3", t), 1)
                for p in range(NHC // 2):
                    R.wait(sc, "PE", ("gmm", t, 2 * p + 1))
                    if p >= 2:
                        R.wait(sc, "DV", ("hh", t, p - 2))
                    elif t >= 1:
                        R.wait(sc, "DV", ("hh", t - 1, NHC // 2 - 2 + p))
                    if p == 0:
                        sc.activation(waste[:, 0:1], cvals[:, 0:1], AF.Silu)
                    R.inc(sc.activation(S["sg"][p % 2][:], S["psG"][p % 2][:], AF.Silu),
                          "AC", ("silu", t, p), 1)
                for tc in range(2):
                    R.wait(sc, "DV", ("var4", t, tc))
                    if tc == 0 and t >= 1:
                        R.wait(sc, "DV", ("outtc", t - 1, 1))
                    if tc == 0:
                        sc.activation(waste[:, 0:1], S["bst"][:, 11:12], AF.Ln,
                                      bias=EPS * 127.0 * 127.0)
                    sc.activation(S["bst"][:, 14:15], S["bst"][:, 11:12], AF.Ln, bias=EPS * 127.0 * 127.0)
                    R.inc(sc.activation(S["bst"][:, 14:15], S["bst"][:, 14:15], AF.Exp, scale=-0.5),
                          "AC", ("rstd4", t, tc), 1)
                R.wait(sc, "DV", ("outtc", t, 0))
                R.wait(sc, "DV", ("outtc", t, 1))
                R.qbarrier(sc, "ST")
                R.inc(sc.dma_start(out[t], S["out_sb"][:]), "ST", ("out_st", t), 16)

            if tensor is not None:
                te = tensor
                pass
                R.wait(te, "DV", ("g", t))
                for b in range(2):
                    if (t, b) != (0, 0):
                        R.wait(te, "DV", ("gTe", t, b - 1) if b > 0 else ("gTe", t - 1, 1))
                    ins = None
                    for ki in range(NKI):
                        ins = te.transpose(S["psT"][:, ki * 128:(ki + 1) * 128],
                                           S["g"][:, b * D + ki * 128: b * D + (ki + 1) * 128], ident[:])
                    R.inc(ins, "PE", ("gTb", t, b), 1)
                R.alias("PE", ("gT_done", t))
                R.wait(te, "DV", ("gTe", t, 0))
                R.wait(te, "DV", ("gTe", t, 1))
                if t == 0:
                    R.wait(te, "WT", ("wgT", NKI - 1))
                    R.wait(te, "WT", ("wuT", NKI - 1))
                for hc in range(NHC):
                    p, half = hc // 2, hc % 2
                    if half == 0:
                        if p >= 2:
                            R.wait(te, "AC", ("silu", t, p - 2))
                            R.wait(te, "DV", ("hh", t, p - 2))
                        elif t >= 1:
                            R.wait(te, "AC", ("silu", t - 1, NHC // 2 - 2 + p))
                            R.wait(te, "DV", ("hh", t - 1, NHC // 2 - 2 + p))
                    ins = None
                    for ki in range(NKI):
                        ins = te.matmul(S["psG"][p % 2][:, half * TB:(half + 1) * TB],
                                        S["wgT"][:, ki * HID + hc * 128: ki * HID + (hc + 1) * 128],
                                        S["gT"][:, ki * TB:(ki + 1) * TB],
                                        start=(ki == 0), stop=(ki == NKI - 1))
                    R.inc(ins, "PE", ("gmm", t, hc), 1)
                    ins = None
                    for ki in range(NKI):
                        ins = te.matmul(S["psU"][p % 2][:, half * TB:(half + 1) * TB],
                                        S["wuT"][:, ki * HID + hc * 128: ki * HID + (hc + 1) * 128],
                                        S["gT"][:, ki * TB:(ki + 1) * TB],
                                        start=(ki == 0), stop=(ki == NKI - 1))
                    R.inc(ins, "PE", ("umm", t, hc), 1)
                if t == 0:
                    R.wait(te, "WT", ("wdT", NHC - 1))
                for tc in range(2):
                    for oh in range(2):
                        did = tc * 2 + oh
                        if (t, did) > (0, 1):
                            pv = (t, did - 2) if did >= 2 else (t - 1, did + 2)
                            R.wait(te, "DV", ("dev", pv[0], pv[1] // 2, pv[1] % 2))
                        ins = None
                        for hc in range(NHC):
                            if tc == 0 and oh == 0 and hc % 2 == 0:
                                R.wait(te, "DV", ("hh", t, hc // 2))
                            ins = te.matmul(S["psDn"][did % 2][:],
                                            S["hh"][:, hc * TB + tc * 128: hc * TB + (tc + 1) * 128],
                                            S["wdT"][:, hc * D + oh * 512: hc * D + (oh + 1) * 512],
                                            start=(hc == 0), stop=(hc == NHC - 1))
                        R.inc(ins, "PE", ("dmm", t, tc, oh), 1)

        def alloc_A(stack, px=""):
            sba = lambda name, shape, dt=F32: stack.enter_context(nc.sbuf_tensor(px + name, shape, dt))
            S = {}
            S["xt"] = [sba("xt0", [128, 5 * D]), sba("xt1", [128, 5 * D])]
            S["wqT"] = sba("wqT", [128, NKI * D], BF16)
            S["wkT"] = sba("wkT", [128, NKI * D], BF16)
            S["wvT"] = sba("wvT", [128, NKI * D], BF16)
            S["woT"] = sba("woT", [128, NKI * D], BF16)
            S["h"] = sba("h", [128, 5 * D], BF16)
            S["hT"] = sba("hT", [128, NKI * 640], BF16)
            S["qTz"] = [sba("qTz0", [128, NKI * TA], BF16), sba("qTz1", [128, NKI * TA], BF16)]
            S["kT"] = sba("kT", [128, NKI * 640], BF16)
            S["vtm"] = sba("vtm", [128, 5 * D], BF16)
            S["p_raw"] = [sba("p_raw0", [128, 1024], BF16), sba("p_raw1", [128, 1024], BF16)]
            S["p_sb"] = [sba("p_sb0", [128, 1024], BF16), sba("p_sb1", [128, 1024], BF16)]
            S["a_sb"] = sba("a_sb", [128, 1024], BF16)
            S["aT"] = sba("aT", [128, NKI * TA], BF16)
            S["y_sb"] = sba("y_sb", [128, 4 * D])
            S["st5"] = sba("st5", [128, 20])
            S["stt"] = sba("stt", [128, 12])
            S["rs"] = sba("rs", [128, 16])
            S["psE"] = stack.enter_context(nc.psum_tensor(px + "psE", [128, 512], F32))
            S["psA"] = stack.enter_context(nc.psum_tensor(px + "psA", [128, 1024], F32))
            S["psB"] = stack.enter_context(nc.psum_tensor(px + "psB", [128, 1024], F32))
            S["psC"] = stack.enter_context(nc.psum_tensor(px + "psC", [128, 1024], F32))
            S["psD"] = stack.enter_context(nc.psum_tensor(px + "psD", [128, 1024], BF16))
            return S

        def alloc_B(stack, px=""):
            sbb = lambda name, shape, dt=F32: stack.enter_context(nc.sbuf_tensor(px + name, shape, dt))
            S = {}
            S["wgT"] = sbb("wgT", [128, NKI * HID], BF16)
            S["wuT"] = sbb("wuT", [128, NKI * HID], BF16)
            S["wdT"] = sbb("wdT", [128, NHC * D], BF16)
            S["yt"] = [sbb("yt0", [128, 2 * D]), sbb("yt1", [128, 2 * D])]
            S["g"] = sbb("g", [128, 2 * D], BF16)
            S["gT"] = sbb("gT", [128, NKI * TB], BF16)
            S["hh"] = sbb("hh", [128, NHC * TB], BF16)
            S["sg"] = [sbb("sg0", [128, 2 * TB]), sbb("sg1", [128, 2 * TB])]
            S["ffn"] = sbb("ffn", [128, 2 * D])
            S["out_sb"] = sbb("out_sb", [128, 2 * D])
            S["bst"] = sbb("bst", [128, 16])
            S["psT"] = stack.enter_context(nc.psum_tensor(px + "psT", [128, 1024], BF16))
            S["psG"] = [stack.enter_context(nc.psum_tensor(px + "psG0", [128, 512], F32)),
                        stack.enter_context(nc.psum_tensor(px + "psG1", [128, 512], F32))]
            S["psU"] = [stack.enter_context(nc.psum_tensor(px + "psU0", [128, 512], F32)),
                        stack.enter_context(nc.psum_tensor(px + "psU1", [128, 512], F32))]
            S["psDn"] = [stack.enter_context(nc.psum_tensor(px + "psDn0", [128, 512], F32)),
                         stack.enter_context(nc.psum_tensor(px + "psDn1", [128, 512], F32))]
            return S

        m = MockEng()
        def per_engine(emit, S):
            for idx in range(5):
                engines = [None] * 5
                engines[idx] = m
                emit(S, tuple(engines))

        with contextlib.ExitStack() as p0a:
            SA0 = alloc_A(p0a, px="z0")
            per_engine(emit_phaseA, SA0)
        if "B" in phases:
            with contextlib.ExitStack() as p0b:
                SB0 = alloc_B(p0b, px="z0")
                per_engine(emit_phaseB, SB0)

        sems = {n: es.enter_context(nc.semaphore(n)) for n in semnames}
        R.start_pass1(sems)

        nums = sorted(s.num for s in sems.values())
        clear_range = range(nums[0], nums[-1] + 1)
        with nc.Block() as blk0:
            @blk0.gpsimd
            def _(g):
                g.dma_reset(clear_range)
                g.sem_clear(clear_range)

        blk = es.enter_context(nc.Block())

        pa = contextlib.ExitStack()
        SA = alloc_A(pa)

        pb_holder = {}

        def engine_main(which):
            def f(eng):
                engines = [None] * 5
                idx = {"sync": 0, "scalar": 1, "vector": 2, "tensor": 3, "gpsimd": 4}[which]
                engines[idx] = eng
                sem_of = {"scalar": "AC", "vector": "DV"}
                if which in sem_of:
                    R.eng_for_sem[sem_of[which]] = eng
                emit_phaseA(SA, tuple(engines))
                if "B" in phases:
                    emit_phaseB(pb_holder["SB"], tuple(engines))
                elif which == "sync":
                    R.final_wait(eng, "ST")
                    R.final_wait(eng, "WC")
            return f

        pa.close()
        pb = contextlib.ExitStack()
        SB = alloc_B(pb)
        pb_holder["SB"] = SB

        blk.sync(engine_main("sync"))
        blk.gpsimd(engine_main("gpsimd"))
        blk.vector(engine_main("vector"))
        blk.scalar(engine_main("scalar"))
        blk.tensor(engine_main("tensor"))
        assert R.cnt == R.totals, ("pass mismatch", {k: (R.cnt.get(k), R.totals.get(k))
                                   for k in set(R.cnt) | set(R.totals)
                                   if R.cnt.get(k) != R.totals.get(k)})
        pb.close()

    return nc


def make_masks():
    i = np.arange(128)[:, None]
    j = np.arange(256)[None, :]
    band = (j >= i + 1) & (j <= i + 128)
    first = band & (j >= 128)

    def to_T(m):
        mT = m.T.astype(np.float32)
        outm = np.zeros((128, 256), dtype=np.float32)
        outm[:, 0:128] = mT[0:128, :]
        outm[:, 128:256] = mT[128:256, :]
        return outm.astype(ml_dtypes.bfloat16)
    return to_T(band), to_T(first)


def kernel(L0_attn_x, L0_attn_Wq, L0_attn_Wk, L0_attn_Wv, L0_attn_Wo,
           L0_ffn_w_gate, L0_ffn_w_up, L0_ffn_w_down):
    x = np.ascontiguousarray(np.asarray(L0_attn_x, dtype=np.float32)).reshape(32768, D)
    mask_stdT, mask_firstT = make_masks()
    ident = np.eye(128, dtype=np.float32).astype(ml_dtypes.bfloat16)

    nc = build_program(DEBUG_DUMP, PHASES)
    in_maps = []
    for c in range(8):
        r0 = c * TOK
        halo = np.zeros((128, D), dtype=np.float32) if c % 4 == 0 else x[r0 - 128:r0]
        xhc = np.concatenate([halo, x[r0:r0 + TOK]], axis=0)
        xt_t = np.stack([xhc[t * TA:t * TA + TA + 128].reshape(5, 128, D)
                         .transpose(1, 0, 2).reshape(128, 5 * D) for t in range(NTA)])
        in_maps.append(dict(
            xh=np.ascontiguousarray(xt_t),
            wq=np.ascontiguousarray(np.asarray(L0_attn_Wq, dtype=np.float32)),
            wk=np.ascontiguousarray(np.asarray(L0_attn_Wk, dtype=np.float32)),
            wv=np.ascontiguousarray(np.asarray(L0_attn_Wv, dtype=np.float32)),
            wo=np.ascontiguousarray(np.asarray(L0_attn_Wo, dtype=np.float32)),
            wg=np.ascontiguousarray(np.asarray(L0_ffn_w_gate, dtype=np.int32)),
            wu=np.ascontiguousarray(np.asarray(L0_ffn_w_up, dtype=np.int32)),
            wd=np.ascontiguousarray(np.asarray(L0_ffn_w_down, dtype=np.int32)),
            ident_in=ident,
            consts_in=np.concatenate([np.tile(np.array([[EPS, EPS * 127.0 * 127.0, float(np.log(DQ)), 0.0]],
                                                   dtype=np.float32), (128, 1)),
                              np.where(np.arange(128)[:, None] < 64, QSCALE, 0.0).astype(np.float32),
                              np.where(np.arange(128)[:, None] >= 64, QSCALE, 0.0).astype(np.float32),
                              np.zeros((128, 2), np.float32)], axis=1),
            mask_first_in=mask_firstT if c % 4 == 0 else mask_stdT,
            mask_std_in=mask_stdT,
        ))
    res = run_bass_kernel_spmd(nc, in_maps, core_ids=list(range(8)))
    parts = []
    for r in res.results:
        o = np.asarray(r["out"])
        o = o.reshape(NTB, 128, 2, D).transpose(0, 2, 1, 3).reshape(TOK, D)
        parts.append(o)
    return np.concatenate(parts, axis=0).reshape(2, 16384, D)


# version 05 (final)
# speedup vs baseline: 1.0721x; 1.0608x over previous
import contextlib
import numpy as np
import ml_dtypes
import concourse.bass as bass
import concourse.mybir as mybir
from concourse.bass_utils import run_bass_kernel_spmd

F32 = mybir.dt.float32
BF16 = mybir.dt.bfloat16
I32 = mybir.dt.int32
AF = mybir.ActivationFunctionType
ALU = mybir.AluOpType
AX = mybir.AxisListType

D = 1024
NH = 16
HD = 64
TOK = 4096
TA = 512
NTA = TOK // TA
TB = 256
NTB = TOK // TB
HID = 3072
NKI = D // 128
NHC = HID // 128
EPS = 1e-5
QSCALE = 1.0 / 8.0
DQ = 1.0 / 127.0

DEBUG_DUMP = None
PHASES = "AB"
STAGE = 7


class MockInst:
    def then_inc(self, *a, **k):
        return self


class MockEng:
    def __getattr__(self, name):
        def f(*a, **k):
            return MockInst()
        return f


DMA_SEMS = {"IN", "WC", "WT", "WU", "ST"}


class Reg:

    def __init__(self):
        self.phase = 0
        self.cnt = {}
        self.miles = {}
        self.pending = {}
        self.totals = None
        self.sems = None
        self.eng_for_sem = {}

    def start_pass1(self, sems):
        for sem, keys in self.pending.items():
            for key in keys:
                self.miles[(sem, key)] = self.cnt.get(sem, 0)
        self.pending = {}
        self.phase = 1
        self.sems = sems
        self.totals = dict(self.cnt)
        self.cnt = {}

    def inc(self, inst, sem, key=None, amt=1):
        self.cnt[sem] = self.cnt.get(sem, 0) + amt
        if self.phase == 0:
            if key is not None:
                assert (sem, key) not in self.miles and \
                    key not in self.pending.get(sem, []), (sem, key)
                if sem in DMA_SEMS:
                    self.pending.setdefault(sem, []).append(key)
                else:
                    self.miles[(sem, key)] = self.cnt[sem]
        else:
            eng = self.eng_for_sem.get(sem)
            if eng is not None:
                eng.drain().then_inc(self.sems[sem], amt)
            else:
                inst.then_inc(self.sems[sem], amt)
        return inst

    def alias(self, sem, key):
        if self.phase == 0:
            assert (sem, key) not in self.miles, (sem, key)
            self.miles[(sem, key)] = self.cnt.get(sem, 0)

    def stable(self, *sems_):
        if self.phase != 0:
            return
        for sem in sems_:
            for key in self.pending.pop(sem, []):
                self.miles[(sem, key)] = self.cnt.get(sem, 0)

    def qbarrier(self, eng, sem):
        if self.phase == 0:
            self.stable(sem)
        else:
            eng.wait_ge(self.sems[sem], self.cnt.get(sem, 0))

    def wait(self, eng, sem, key):
        if self.phase == 0:
            return
        val = self.miles[(sem, key)]
        assert val <= self.totals.get(sem, 0), ("wait exceeds total", sem, key, val, self.totals.get(sem, 0))
        eng.wait_ge(self.sems[sem], val)

    def final_wait(self, eng, sem):
        if self.phase == 0:
            return
        tot = self.totals.get(sem, 0)
        if tot:
            eng.wait_ge(self.sems[sem], tot)


def bcast(ap, shape):
    return ap.broadcast_to(shape)


def build_program(debug_dump=None, phases="AB"):
    nc = bass.Bass("TRN2", target_bir_lowering=False, debug=False,
                   detect_race_conditions=False)

    xh = nc.dram_tensor("xh", [NTA, 128, 5 * D], F32, kind="ExternalInput").ap()
    wq = nc.dram_tensor("wq", [D, D], F32, kind="ExternalInput").ap()
    wk = nc.dram_tensor("wk", [D, D], F32, kind="ExternalInput").ap()
    wv = nc.dram_tensor("wv", [D, D], F32, kind="ExternalInput").ap()
    wo = nc.dram_tensor("wo", [D, D], F32, kind="ExternalInput").ap()
    wg = nc.dram_tensor("wg", [HID, D], I32, kind="ExternalInput").ap()
    wu = nc.dram_tensor("wu", [HID, D], I32, kind="ExternalInput").ap()
    wd = nc.dram_tensor("wd", [D, HID], I32, kind="ExternalInput").ap()
    ident_in = nc.dram_tensor("ident_in", [128, 128], BF16, kind="ExternalInput").ap()
    consts_in = nc.dram_tensor("consts_in", [128, 8], F32, kind="ExternalInput").ap()
    mask_first_in = nc.dram_tensor("mask_first_in", [128, 256], BF16, kind="ExternalInput").ap()
    mask_std_in = nc.dram_tensor("mask_std_in", [128, 256], BF16, kind="ExternalInput").ap()

    out = nc.dram_tensor("out", [NTB, 128, 2 * D], F32, kind="ExternalOutput").ap()
    dbg = None
    if debug_dump:
        ddt = F32 if debug_dump == "y" else BF16
        dbg = nc.dram_tensor("dbg", [NTA, 128, 5 * D], ddt, kind="ExternalOutput").ap()

    ybuf = nc.dram_tensor("ybuf", [NTA, 128, 4 * D], F32, kind="Internal").ap()
    wbf = {n: nc.dram_tensor(n + "_bf", list(shape), BF16, kind="Internal").ap()
           for n, shape in [("wq", (D, D)), ("wk", (D, D)), ("wv", (D, D)), ("wo", (D, D)),
                            ("wg", (HID, D)), ("wu", (HID, D)), ("wd", (D, HID))]}

    R = Reg()

    es = contextlib.ExitStack()
    with es:
        sbp = lambda name, shape, dt=F32: es.enter_context(nc.sbuf_tensor(name, shape, dt))
        ident = sbp("ident", [128, 128], BF16)
        mask_first = sbp("mask_first", [128, 256], BF16)
        mask_std = sbp("mask_std", [128, 256], BF16)
        ones_bf = sbp("ones_bf", [128, 1], BF16)
        waste = sbp("waste", [128, 8], F32)
        bn12 = sbp("bn12", [128, 12], F32)
        cvals = sbp("cvals", [128, 8], F32)
        nc.const_aps.register(F32, EPS, cvals[:, 0:1])
        nc.const_aps.register(F32, EPS * 127.0 * 127.0, cvals[:, 1:2])
        nc.const_aps.register(F32, float(np.log(DQ)), cvals[:, 2:3])
        semnames = ["IN", "WC", "WT", "WU", "PE", "DV", "AC", "ST"]

        def emit_phaseA(S, engines):
            sync, scalar, vector, tensor, gpsimd = engines

            if gpsimd is not None:
                g = gpsimd
                for i, (src, dst) in enumerate([(wq, wbf["wq"]), (wk, wbf["wk"]),
                                                (wv, wbf["wv"]), (wo, wbf["wo"])]):
                    R.inc(g.dma_start(dst, src), "WC", ("attn_cast", i), 16)
                    R.qbarrier(g, "WC")
                for i, (src, dst) in enumerate([(wg, wbf["wg"]), (wu, wbf["wu"]), (wd, wbf["wd"])]):
                    R.inc(g.dma_start(dst, src), "WC", ("ffn_cast", i), 16)
                    R.qbarrier(g, "WC")

            if sync is not None:
                s = sync
                R.inc(s.dma_start(ident[:], ident_in), "IN", ("const", 0), 16)
                R.inc(s.dma_start(mask_first[:], mask_first_in), "IN", ("const", 1), 16)
                R.inc(s.dma_start(mask_std[:], mask_std_in), "IN", ("const", 2), 16)
                R.inc(s.dma_start(cvals[:], consts_in), "IN", ("const", 3), 16)
                for i, (name, dstT) in enumerate([("wq", S["wqT"]), ("wk", S["wkT"]),
                                                  ("wv", S["wvT"]), ("wo", S["woT"])]):
                    R.qbarrier(s, "IN")
                    R.qbarrier(s, "WT")
                    R.wait(s, "WC", ("attn_cast", i))
                    for ki in range(NKI):
                        R.inc(s.dma_start(dstT[:, ki * D:(ki + 1) * D],
                                          wbf[name][:, ki * 128:(ki + 1) * 128], transpose=True),
                              "WT", ("attn_wT", i, ki), 16)
                for t in range(NTA):
                    if t >= 2:
                        R.qbarrier(s, "IN")
                        if STAGE >= 7:
                            R.wait(s, "DV", ("y_done", t - 2))
                    R.inc(s.dma_start(S["xt"][t % 2][:], xh[t]), "IN", ("x", t), 16)

            for t in range(NTA):
                emit_A_tile(S, engines, t)

        def emit_A_tile(S, engines, t):
            sync, scalar, vector, tensor, gpsimd = engines
            xt = S["xt"][t % 2]
            first_chunk_tile = (t == 0)

            if vector is not None:
                v = vector
                if t == 0:
                    R.wait(v, "IN", ("const", 2))
                    v.memset(S["psE"][:, 0:32], 1.0)
                    R.inc(v.memset(ones_bf[:], 1.0), "DV", ("ones",), 1)
                R.wait(v, "IN", ("x", t))
                if t >= 1:
                    R.wait(v, "PE", ("hT_done", t - 1))
                for b in range(5):
                    v.bn_stats(bn12[:, 0:6], xt[:, b * D:b * D + 512])
                    v.bn_stats(bn12[:, 6:12], xt[:, b * D + 512:(b + 1) * D])
                    v.drain()
                    v.bn_aggr(S["st5"][:, 2 * b:2 * b + 2], bn12[:])
                v.drain()
                R.inc(v.tensor_copy(S["stt"][:, 0:1], S["st5"][:, 0:1]), "DV", ("var1", t), 1)
                R.wait(v, "AC", ("rstd1", t))
                ins = None
                for b in range(5):
                    ins = v.tensor_scalar(S["h"][:, b * D:(b + 1) * D], xt[:, b * D:(b + 1) * D],
                                          S["st5"][:, 2 * b:2 * b + 1], S["st5"][:, 10 + b:11 + b],
                                          ALU.subtract, ALU.mult)
                R.inc(ins, "DV", ("h", t), 1)
                if STAGE < 2:
                    return
                for b in range(5):
                    R.wait(v, "PE", ("hTb", t, b))
                    dst = S["hT"][:].rearrange("p (k n) -> p k n", n=640)[:, :, b * 128:(b + 1) * 128]
                    R.inc(v.tensor_copy(dst, S["psD"][:].rearrange("p (k n) -> p k n", n=128)),
                          "DV", ("hTe", t, b), 1)
                if STAGE < 3:
                    return
                for b in range(5):
                    for oh in range(2):
                        R.wait(v, "PE", ("vmm", t, b, oh))
                        R.inc(v.tensor_copy(S["vtm"][:, b * D + oh * 512: b * D + (oh + 1) * 512],
                                            S["psA"][:, oh * 512:(oh + 1) * 512]),
                              "DV", ("vev", t, b, oh), 1)
                if STAGE < 4:
                    return
                for qc in range(4):
                    for g4 in range(4):
                        gid = qc * 4 + g4
                        R.wait(v, "AC", ("exp", t, qc, g4))
                        if (t, gid) > (0, 1) and STAGE >= 5:
                            pq, pg = ((qc, g4 - 2) if g4 >= 2 else (qc - 1, g4 + 2))
                            if pq >= 0:
                                R.wait(v, "PE", ("avg", t, pq, pg))
                            elif t > 0:
                                R.wait(v, "PE", ("avg", t - 1, 3, pg))
                        mk = mask_first if (first_chunk_tile and qc == 0) else mask_std
                        mk4 = mk[:].rearrange("p (j q) -> p j q", j=2).unsqueeze(1)
                        pr = S["p_raw"][gid % 2]
                        pm = S["p_sb"][gid % 2]
                        R.inc(v.tensor_tensor(pm[:].rearrange("p (g j q) -> p g j q", g=4, j=2),
                                              pr[:].rearrange("p (g j q) -> p g j q", g=4, j=2),
                                              bcast(mk4, [128, 4, 2, 128]), ALU.mult),
                              "DV", ("mask", t, qc, g4), 1)
                    if STAGE < 5:
                        continue
                    R.wait(v, "PE", ("avdone", t, qc))
                    psE_v = S["psE"][:, 0:32].rearrange("p (h k) -> p h k", k=2)[:, :, 0:1]
                    v.reciprocal(S["rs"][:, 0:16].unsqueeze(2), psE_v)
                    v.drain()
                    ins = None
                    for bk in range(2):
                        rs3 = bcast(S["rs"][:, bk * 8:(bk + 1) * 8].unsqueeze(2), [128, 8, 64])
                        ins = v.tensor_tensor(
                            S["a_sb"][:, bk * 512:(bk + 1) * 512].rearrange("p (h d) -> p h d", d=64),
                            S["psA"][:, bk * 512:(bk + 1) * 512].rearrange("p (h d) -> p h d", d=64),
                            rs3, ALU.mult)
                    R.inc(ins, "DV", ("aev", t, qc), 1)
                    R.wait(v, "PE", ("aTt", t, qc))
                    dst = S["aT"][:].rearrange("p (k n) -> p k n", n=TA)[:, :, qc * 128:(qc + 1) * 128]
                    R.inc(v.tensor_copy(dst, S["psD"][:].rearrange("p (k n) -> p k n", n=128)),
                          "DV", ("aTe", t, qc), 1)
                if STAGE < 7:
                    return
                if t >= 1:
                    R.wait(v, "ST", ("y_st", t - 1))
                for qc in range(4):
                    pso = S["psB"] if qc % 2 == 0 else S["psC"]
                    R.wait(v, "PE", ("wo", t, qc))
                    v.bn_stats(bn12[:, 0:6], pso[:, 0:512])
                    v.bn_stats(bn12[:, 6:12], pso[:, 512:1024])
                    v.drain()
                    R.inc(v.bn_aggr(S["stt"][:, 1:3], bn12[:]), "DV", ("var2", t, qc), 1)
                    R.wait(v, "AC", ("rstd2", t, qc))
                    v.tensor_scalar(S["y_sb"][:, qc * D:(qc + 1) * D], pso[:],
                                    S["stt"][:, 1:2], S["stt"][:, 5:6], ALU.subtract, ALU.mult)
                    ins = v.tensor_tensor(S["y_sb"][:, qc * D:(qc + 1) * D],
                                          S["y_sb"][:, qc * D:(qc + 1) * D],
                                          xt[:, (1 + qc) * D:(2 + qc) * D], ALU.add)
                    R.inc(ins, "DV", ("y", t, qc), 1)
                R.alias("DV", ("y_done", t))

            if scalar is not None:
                sc = scalar
                if t == 0:
                    R.wait(sc, "IN", ("const", 3))
                    sc.activation(waste[:], cvals[:, 0:8], AF.Square)
                    sc.drain()
                R.wait(sc, "DV", ("var1", t))
                if t >= 1:
                    R.wait(sc, "DV", ("h", t - 1))
                var_v = S["st5"][:, 0:10].rearrange("p (b k) -> p b k", k=2)[:, :, 1:2]
                rstd_v = S["st5"][:, 10:15].unsqueeze(2)
                sc.activation(rstd_v, var_v, AF.Ln, bias=EPS)
                R.inc(sc.activation(rstd_v, rstd_v, AF.Exp, scale=-0.5),
                      "AC", ("rstd1", t), 1)
                if STAGE < 3:
                    return
                for oc in range(NKI):
                    R.wait(sc, "PE", ("qmm", t, oc))
                    sc.activation(S["qTz"][0][:, oc * TA:(oc + 1) * TA],
                                  S["psA"][:, (oc % 2) * 512:(oc % 2 + 1) * 512],
                                  AF.Copy, scale=cvals[:, 4:5])
                    R.inc(sc.activation(S["qTz"][1][:, oc * TA:(oc + 1) * TA],
                                        S["psA"][:, (oc % 2) * 512:(oc % 2 + 1) * 512],
                                        AF.Copy, scale=cvals[:, 5:6]),
                          "AC", ("qev", t, oc), 1)
                for oc in range(NKI):
                    for part, (c0, cn) in enumerate([(0, 512), (512, 128)]):
                        R.wait(sc, "PE", ("kmm", t, oc, part))
                        R.inc(sc.activation(S["kT"][:, oc * 640 + c0: oc * 640 + c0 + cn],
                                            S["psB"][:, c0:c0 + cn], AF.Copy),
                              "AC", ("kev", t, oc, part), 1)
                if STAGE < 4:
                    return
                for qc in range(4):
                    for g4 in range(4):
                        gid = qc * 4 + g4
                        R.wait(sc, "PE", ("sT", t, qc, g4))
                        if (t, gid) > (0, 1):
                            pq, pg = ((qc, g4 - 2) if g4 >= 2 else (qc - 1, g4 + 2))
                            if pq >= 0:
                                R.wait(sc, "DV", ("mask", t, pq, pg))
                            elif t > 0:
                                R.wait(sc, "DV", ("mask", t - 1, 3, pg))
                        psS = S["psB"] if gid % 2 == 0 else S["psC"]
                        pr = S["p_raw"][gid % 2]
                        sc.activation(pr[:, 0:512], psS[:, 0:512], AF.Exp)
                        R.inc(sc.activation(pr[:, 512:1024], psS[:, 512:1024], AF.Exp),
                              "AC", ("exp", t, qc, g4), 1)
                if STAGE < 7:
                    return
                for qc in range(4):
                    R.wait(sc, "DV", ("var2", t, qc))
                    if qc == 0 and t >= 1:
                        R.wait(sc, "DV", ("y", t - 1, 3))
                    sc.activation(S["stt"][:, 5:6], S["stt"][:, 2:3], AF.Ln, bias=EPS)
                    R.inc(sc.activation(S["stt"][:, 5:6], S["stt"][:, 5:6], AF.Exp, scale=-0.5),
                          "AC", ("rstd2", t, qc), 1)
                for qc in range(4):
                    R.wait(sc, "DV", ("y", t, qc))
                R.qbarrier(sc, "ST")
                R.inc(sc.dma_start(ybuf[t], S["y_sb"][:]), "ST", ("y_st", t), 16)
                if DEBUG_DUMP == "y":
                    R.inc(sc.dma_start(dbg[t][:, 0:4 * D], S["y_sb"][:]), "ST", ("dbg", t), 16)

            if scalar is not None and DEBUG_DUMP in ("v", "attn", "h"):
                g = scalar
                if DEBUG_DUMP == "h":
                    R.wait(g, "DV", ("h", t))
                    R.inc(g.dma_start(dbg[t][:, 0:5 * D - 64], S["h"][:, 0:5 * D - 64]),
                          "ST", ("dbg", t), 16)
                    R.inc(g.dma_start(dbg[t][:, 5 * D - 40:5 * D].bitcast(F32),
                                      S["st5"][:]), "ST", ("dbgs", t), 16)
                if DEBUG_DUMP == "v":
                    R.wait(g, "DV", ("vev", t, 4, 1))
                    R.inc(g.dma_start(dbg[t], S["vtm"][:]), "ST", ("dbg", t), 16)
                if DEBUG_DUMP == "attn":
                    for qc in range(4):
                        R.wait(g, "DV", ("aev", t, qc))
                        R.wait(g, "PE", ("aTt", t, qc))
                        R.inc(g.dma_start(dbg[t][:, qc * D:(qc + 1) * D], S["a_sb"][:, 0:1024]),
                              "ST", ("dbg", t, qc), 16)

            if tensor is not None:
                te = tensor
                if STAGE < 2:
                    return
                if t == 0:
                    R.wait(te, "IN", ("const", 0))
                    R.wait(te, "DV", ("ones",))
                R.wait(te, "DV", ("h", t))
                for b in range(5):
                    if (t, b) != (0, 0):
                        R.wait(te, "DV", ("hTe", t, b - 1) if b > 0 else ("aTe", t - 1, 3))
                    ins = None
                    for ki in range(NKI):
                        ins = te.transpose(S["psD"][:, ki * 128:(ki + 1) * 128],
                                           S["h"][:, b * D + ki * 128: b * D + (ki + 1) * 128], ident[:])
                    R.inc(ins, "PE", ("hTb", t, b), 1)
                R.alias("PE", ("hT_done", t))
                if STAGE < 3:
                    return
                for b in range(5):
                    R.wait(te, "DV", ("hTe", t, b))
                if t == 0:
                    R.wait(te, "WT", ("attn_wT", 0, NKI - 1))
                for oc in range(NKI):
                    if oc >= 2:
                        R.wait(te, "AC", ("qev", t, oc - 2))
                    elif t > 0:
                        R.wait(te, "DV", ("aev", t - 1, 3))
                    ins = None
                    for ki in range(NKI):
                        ins = te.matmul(S["psA"][:, (oc % 2) * 512:(oc % 2 + 1) * 512],
                                        S["wqT"][:, ki * D + oc * 128: ki * D + (oc + 1) * 128],
                                        S["hT"][:, ki * 640 + 128: ki * 640 + 640],
                                        start=(ki == 0), stop=(ki == NKI - 1))
                    R.inc(ins, "PE", ("qmm", t, oc), 1)
                if t == 0:
                    R.wait(te, "WT", ("attn_wT", 1, NKI - 1))
                for oc in range(NKI):
                    if oc >= 1:
                        R.wait(te, "AC", ("kev", t, oc - 1, 1))
                    elif t > 0:
                        R.wait(te, "DV", ("y", t - 1, 2))
                        R.wait(te, "DV", ("y", t - 1, 3))
                    for part, (c0, cn) in enumerate([(0, 512), (512, 128)]):
                        ins = None
                        for ki in range(NKI):
                            ins = te.matmul(S["psB"][:, c0:c0 + cn],
                                            S["wkT"][:, ki * D + oc * 128: ki * D + (oc + 1) * 128],
                                            S["hT"][:, ki * 640 + c0: ki * 640 + c0 + cn],
                                            start=(ki == 0), stop=(ki == NKI - 1))
                        R.inc(ins, "PE", ("kmm", t, oc, part), 1)
                if t == 0:
                    R.wait(te, "WT", ("attn_wT", 2, NKI - 1))
                for b in range(5):
                    for oh in range(2):
                        if b == 0:
                            R.wait(te, "AC", ("qev", t, 6 + oh))
                        else:
                            R.wait(te, "DV", ("vev", t, b - 1, oh))
                        ins = None
                        for ki in range(NKI):
                            ins = te.matmul(S["psA"][:, oh * 512:(oh + 1) * 512],
                                            S["hT"][:, ki * 640 + b * 128: ki * 640 + (b + 1) * 128],
                                            S["wvT"][:, ki * D + oh * 512: ki * D + (oh + 1) * 512],
                                            start=(ki == 0), stop=(ki == NKI - 1))
                        R.inc(ins, "PE", ("vmm", t, b, oh), 1)
                if STAGE < 4:
                    return
                for qc in range(4):
                    for g4 in range(4):
                        gid = qc * 4 + g4
                        if gid >= 2:
                            pq, pg = ((qc, g4 - 2) if g4 >= 2 else (qc - 1, g4 + 2))
                            R.wait(te, "AC", ("exp", t, pq, pg))
                        elif gid == 0:
                            R.wait(te, "AC", ("kev", t, NKI - 1, 1))
                        psS = S["psB"] if gid % 2 == 0 else S["psC"]
                        ins = None
                        for hh in range(4):
                            hg = g4 * 4 + hh
                            oc = hg // 2
                            qtz = S["qTz"][hg % 2]
                            for jc in range(2):
                                ins = te.matmul(
                                    psS[:, hh * 256 + jc * 128: hh * 256 + (jc + 1) * 128],
                                    S["kT"][:, oc * 640 + (qc + jc) * 128: oc * 640 + (qc + jc + 1) * 128],
                                    qtz[:, oc * TA + qc * 128: oc * TA + (qc + 1) * 128],
                                    start=True, stop=True)
                        R.inc(ins, "PE", ("sT", t, qc, g4), 1)
                    if STAGE < 5:
                        continue
                    if qc == 0:
                        R.wait(te, "DV", ("vev", t, 4, 1))
                    else:
                        R.wait(te, "DV", ("aev", t, qc - 1))
                    for g4 in range(4):
                        R.wait(te, "DV", ("mask", t, qc, g4))
                        pm = S["p_sb"][(qc * 4 + g4) % 2]
                        ins = None
                        for hh in range(4):
                            hg = g4 * 4 + hh
                            for jc in range(2):
                                psl = pm[:, hh * 256 + jc * 128: hh * 256 + (jc + 1) * 128]
                                te.matmul(S["psA"][:, hg * 64:(hg + 1) * 64], psl,
                                          S["vtm"][:, (qc + jc) * D + hg * HD: (qc + jc) * D + (hg + 1) * HD],
                                          start=(jc == 0), stop=(jc == 1))
                                ins = te.matmul(S["psE"][:, 2 * hg:2 * hg + 1], psl, ones_bf[:],
                                                start=(jc == 0), stop=(jc == 1))
                        R.inc(ins, "PE", ("avg", t, qc, g4), 1)
                    R.alias("PE", ("avdone", t, qc))
                    R.wait(te, "DV", ("aev", t, qc))
                    R.wait(te, "DV", ("hTe", t, 4) if qc == 0 else ("aTe", t, qc - 1))
                    ins = None
                    for ki in range(NKI):
                        ins = te.transpose(S["psD"][:, ki * 128:(ki + 1) * 128],
                                           S["a_sb"][:, ki * 128:(ki + 1) * 128], ident[:])
                    R.inc(ins, "PE", ("aTt", t, qc), 1)
                if STAGE < 6:
                    return
                if t == 0:
                    R.wait(te, "WT", ("attn_wT", 3, NKI - 1))
                for qc in range(4):
                    pso = S["psB"] if qc % 2 == 0 else S["psC"]
                    R.wait(te, "DV", ("aTe", t, qc))
                    R.wait(te, "AC", ("exp", t, 3, 2 + (qc % 2)))
                    if qc >= 2 and STAGE >= 7:
                        R.wait(te, "DV", ("y", t, qc - 2))
                    ins = None
                    for oh in range(2):
                        for ki in range(NKI):
                            ins = te.matmul(pso[:, oh * 512:(oh + 1) * 512],
                                            S["aT"][:, ki * TA + qc * 128: ki * TA + (qc + 1) * 128],
                                            S["woT"][:, ki * D + oh * 512: ki * D + (oh + 1) * 512],
                                            start=(ki == 0), stop=(ki == NKI - 1))
                    R.inc(ins, "PE", ("wo", t, qc), 1)

        def v_ln3_g(S, v, u):
            ytu = S["yt"][u % 2]
            R.wait(v, "IN", ("y", u))
            if u >= 1:
                R.wait(v, "PE", ("gT_done", u - 1))
            for b in range(2):
                v.bn_stats(bn12[:, 0:6], ytu[:, b * D:b * D + 512])
                v.bn_stats(bn12[:, 6:12], ytu[:, b * D + 512:(b + 1) * D])
                v.drain()
                v.bn_aggr(S["bst"][:, 2 * b:2 * b + 2], bn12[:])
            v.drain()
            R.inc(v.tensor_copy(S["bst"][:, 6:7], S["bst"][:, 0:1]), "DV", ("var3", u), 1)
            R.wait(v, "AC", ("rstd3", u))
            ins = None
            for b in range(2):
                ins = v.tensor_scalar(S["g"][:, b * D:(b + 1) * D], ytu[:, b * D:(b + 1) * D],
                                      S["bst"][:, 2 * b:2 * b + 1], S["bst"][:, 8 + b:9 + b],
                                      ALU.subtract, ALU.mult)
            R.inc(ins, "DV", ("g", u), 1)

        def v_gT_evac(S, v, u, b):
            R.wait(v, "PE", ("gTb", u, b))
            dst = S["gT"][:].rearrange("p (k n) -> p k n", n=TB)[:, :, b * 128:(b + 1) * 128]
            R.inc(v.tensor_copy(dst, S["psT"][:].rearrange("p (k n) -> p k n", n=128)),
                  "DV", ("gTe", u, b), 1)

        def sc_rstd3(S, sc, u):
            R.wait(sc, "DV", ("var3", u))
            if u >= 1:
                R.wait(sc, "DV", ("g", u - 1))
            var3_v = S["bst"][:, 0:4].rearrange("p (b k) -> p b k", k=2)[:, :, 1:2]
            rstd3_v = S["bst"][:, 8:10].unsqueeze(2)
            sc.activation(rstd3_v, var3_v, AF.Ln, bias=EPS)
            R.inc(sc.activation(rstd3_v, rstd3_v, AF.Exp, scale=-0.5,
                                bias=float(np.log(DQ))),
                  "AC", ("rstd3", u), 1)

        def emit_phaseB(S, engines):
            sync, scalar, vector, tensor, gpsimd = engines

            if vector is not None:
                v_ln3_g(S, vector, 0)
                v_gT_evac(S, vector, 0, 0)
                v_gT_evac(S, vector, 0, 1)
            if scalar is not None:
                sc_rstd3(S, scalar, 0)

            if sync is not None:
                s = sync
                R.qbarrier(s, "IN")
                R.qbarrier(s, "WT")
                R.wait(s, "DV", ("y_done", NTA - 1))
                R.wait(s, "ST", ("y_st", NTA - 1))
                R.wait(s, "WC", ("ffn_cast", 0))
                for ki in range(NKI):
                    R.inc(s.dma_start(S["wgT"][:, ki * HID:(ki + 1) * HID],
                                      wbf["wg"][:, ki * 128:(ki + 1) * 128], transpose=True),
                          "WT", ("wgT", ki), 16)
                R.qbarrier(s, "WT")
                R.wait(s, "WC", ("ffn_cast", 1))
                for ki in range(NKI):
                    R.inc(s.dma_start(S["wuT"][:, ki * HID:(ki + 1) * HID],
                                      wbf["wu"][:, ki * 128:(ki + 1) * 128], transpose=True),
                          "WT", ("wuT", ki), 16)
                R.qbarrier(s, "WT")
                R.wait(s, "WC", ("ffn_cast", 2))
                for hc in range(NHC):
                    R.inc(s.dma_start(S["wdT"][:, hc * D:(hc + 1) * D],
                                      wbf["wd"][:, hc * 128:(hc + 1) * 128], transpose=True),
                          "WT", ("wdT", hc), 16)
                for t in range(NTB):
                    R.qbarrier(s, "IN")
                    if t == 0:
                        R.qbarrier(s, "WT")
                    R.wait(s, "ST", ("y_st", min((t * TB + TB - 1) // TA, NTA - 1)))
                    if t >= 2:
                        R.wait(s, "DV", ("outv", t - 2))
                    srcy = ybuf[t // 2][:, (t % 2) * 2 * D:((t % 2) + 1) * 2 * D]
                    R.inc(s.dma_start(S["yt"][t % 2][:], srcy), "IN", ("y", t), 16)

            for t in range(NTB):
                emit_B_tile(S, engines, t)

            if sync is not None:
                R.final_wait(sync, "ST")

        def emit_B_tile(S, engines, t):
            sync, scalar, vector, tensor, gpsimd = engines
            yt = S["yt"][t % 2]

            if vector is not None:
                v = vector
                for p in range(NHC // 2):
                    R.wait(v, "AC", ("silu", t, p))
                    R.wait(v, "PE", ("umm", t, 2 * p + 1))
                    psu = S["psU"][p % 2]
                    R.inc(v.tensor_tensor(S["hh"][:, 2 * p * TB:(2 * p + 2) * TB],
                                          psu[:], S["sg"][p % 2][:], ALU.mult),
                          "DV", ("hh", t, p), 1)
                if t + 1 < NTB:
                    v_ln3_g(S, v, t + 1)
                for tc in range(2):
                    for oh in range(2):
                        R.wait(v, "PE", ("dmm", t, tc, oh))
                        R.inc(v.tensor_copy(S["ffn"][:, tc * D + oh * 512: tc * D + (oh + 1) * 512],
                                            S["psDn"][(tc * 2 + oh) % 2][:]),
                              "DV", ("dev", t, tc, oh), 1)
                        if oh == 1 and t + 1 < NTB:
                            v_gT_evac(S, v, t + 1, tc)
                if t >= 1:
                    R.wait(v, "ST", ("out_st", t - 1))
                for tc in range(2):
                    v.bn_stats(bn12[:, 0:6], S["ffn"][:, tc * D:tc * D + 512])
                    v.bn_stats(bn12[:, 6:12], S["ffn"][:, tc * D + 512:(tc + 1) * D])
                    v.drain()
                    R.inc(v.bn_aggr(S["bst"][:, 10:12], bn12[:]), "DV", ("var4", t, tc), 1)
                    R.wait(v, "AC", ("rstd4", t, tc))
                    v.tensor_scalar(S["out_sb"][:, tc * D:(tc + 1) * D], S["ffn"][:, tc * D:(tc + 1) * D],
                                    S["bst"][:, 10:11], S["bst"][:, 14:15], ALU.subtract, ALU.mult)
                    ins = v.tensor_tensor(S["out_sb"][:, tc * D:(tc + 1) * D],
                                          S["out_sb"][:, tc * D:(tc + 1) * D],
                                          yt[:, tc * D:(tc + 1) * D], ALU.add)
                    R.inc(ins, "DV", ("outtc", t, tc), 1)
                R.alias("DV", ("outv", t))

            if scalar is not None:
                sc = scalar
                for p in range(NHC // 2):
                    R.wait(sc, "PE", ("gmm", t, 2 * p + 1))
                    if p >= 2:
                        R.wait(sc, "DV", ("hh", t, p - 2))
                    elif t >= 1:
                        R.wait(sc, "DV", ("hh", t - 1, NHC // 2 - 2 + p))
                    if p == 0:
                        sc.activation(waste[:, 0:1], cvals[:, 0:1], AF.Silu)
                    R.inc(sc.activation(S["sg"][p % 2][:], S["psG"][p % 2][:], AF.Silu),
                          "AC", ("silu", t, p), 1)
                if t + 1 < NTB:
                    sc_rstd3(S, sc, t + 1)
                for tc in range(2):
                    R.wait(sc, "DV", ("var4", t, tc))
                    if tc == 0 and t >= 1:
                        R.wait(sc, "DV", ("outtc", t - 1, 1))
                    if tc == 0:
                        sc.activation(waste[:, 0:1], S["bst"][:, 11:12], AF.Ln,
                                      bias=EPS * 127.0 * 127.0)
                    sc.activation(S["bst"][:, 14:15], S["bst"][:, 11:12], AF.Ln, bias=EPS * 127.0 * 127.0)
                    R.inc(sc.activation(S["bst"][:, 14:15], S["bst"][:, 14:15], AF.Exp, scale=-0.5),
                          "AC", ("rstd4", t, tc), 1)
                R.wait(sc, "DV", ("outtc", t, 0))
                R.wait(sc, "DV", ("outtc", t, 1))
                R.qbarrier(sc, "ST")
                R.inc(sc.dma_start(out[t], S["out_sb"][:]), "ST", ("out_st", t), 16)

            if tensor is not None:
                te = tensor
                pass
                R.wait(te, "DV", ("g", t))
                for b in range(2):
                    if (t, b) != (0, 0):
                        R.wait(te, "DV", ("gTe", t, b - 1) if b > 0 else ("gTe", t - 1, 1))
                    ins = None
                    for ki in range(NKI):
                        ins = te.transpose(S["psT"][:, ki * 128:(ki + 1) * 128],
                                           S["g"][:, b * D + ki * 128: b * D + (ki + 1) * 128], ident[:])
                    R.inc(ins, "PE", ("gTb", t, b), 1)
                R.alias("PE", ("gT_done", t))
                R.wait(te, "DV", ("gTe", t, 0))
                R.wait(te, "DV", ("gTe", t, 1))
                if t == 0:
                    R.wait(te, "WT", ("wgT", NKI - 1))
                    R.wait(te, "WT", ("wuT", NKI - 1))
                for hc in range(NHC):
                    p, half = hc // 2, hc % 2
                    if half == 0:
                        if p >= 2:
                            R.wait(te, "AC", ("silu", t, p - 2))
                            R.wait(te, "DV", ("hh", t, p - 2))
                        elif t >= 1:
                            R.wait(te, "AC", ("silu", t - 1, NHC // 2 - 2 + p))
                            R.wait(te, "DV", ("hh", t - 1, NHC // 2 - 2 + p))
                    ins = None
                    for ki in range(NKI):
                        ins = te.matmul(S["psG"][p % 2][:, half * TB:(half + 1) * TB],
                                        S["wgT"][:, ki * HID + hc * 128: ki * HID + (hc + 1) * 128],
                                        S["gT"][:, ki * TB:(ki + 1) * TB],
                                        start=(ki == 0), stop=(ki == NKI - 1))
                    R.inc(ins, "PE", ("gmm", t, hc), 1)
                    ins = None
                    for ki in range(NKI):
                        ins = te.matmul(S["psU"][p % 2][:, half * TB:(half + 1) * TB],
                                        S["wuT"][:, ki * HID + hc * 128: ki * HID + (hc + 1) * 128],
                                        S["gT"][:, ki * TB:(ki + 1) * TB],
                                        start=(ki == 0), stop=(ki == NKI - 1))
                    R.inc(ins, "PE", ("umm", t, hc), 1)
                if t == 0:
                    R.wait(te, "WT", ("wdT", NHC - 1))
                for tc in range(2):
                    for oh in range(2):
                        did = tc * 2 + oh
                        if (t, did) > (0, 1):
                            pv = (t, did - 2) if did >= 2 else (t - 1, did + 2)
                            R.wait(te, "DV", ("dev", pv[0], pv[1] // 2, pv[1] % 2))
                        ins = None
                        for hc in range(NHC):
                            if tc == 0 and oh == 0 and hc % 2 == 0:
                                R.wait(te, "DV", ("hh", t, hc // 2))
                            ins = te.matmul(S["psDn"][did % 2][:],
                                            S["hh"][:, hc * TB + tc * 128: hc * TB + (tc + 1) * 128],
                                            S["wdT"][:, hc * D + oh * 512: hc * D + (oh + 1) * 512],
                                            start=(hc == 0), stop=(hc == NHC - 1))
                        R.inc(ins, "PE", ("dmm", t, tc, oh), 1)

        def alloc_A(stack, px=""):
            sba = lambda name, shape, dt=F32: stack.enter_context(nc.sbuf_tensor(px + name, shape, dt))
            S = {}
            S["xt"] = [sba("xt0", [128, 5 * D]), sba("xt1", [128, 5 * D])]
            S["wqT"] = sba("wqT", [128, NKI * D], BF16)
            S["wkT"] = sba("wkT", [128, NKI * D], BF16)
            S["wvT"] = sba("wvT", [128, NKI * D], BF16)
            S["woT"] = sba("woT", [128, NKI * D], BF16)
            S["h"] = sba("h", [128, 5 * D], BF16)
            S["hT"] = sba("hT", [128, NKI * 640], BF16)
            S["qTz"] = [sba("qTz0", [128, NKI * TA], BF16), sba("qTz1", [128, NKI * TA], BF16)]
            S["kT"] = sba("kT", [128, NKI * 640], BF16)
            S["vtm"] = sba("vtm", [128, 5 * D], BF16)
            S["p_raw"] = [sba("p_raw0", [128, 1024], BF16), sba("p_raw1", [128, 1024], BF16)]
            S["p_sb"] = [sba("p_sb0", [128, 1024], BF16), sba("p_sb1", [128, 1024], BF16)]
            S["a_sb"] = sba("a_sb", [128, 1024], BF16)
            S["aT"] = sba("aT", [128, NKI * TA], BF16)
            S["y_sb"] = sba("y_sb", [128, 4 * D])
            S["st5"] = sba("st5", [128, 20])
            S["stt"] = sba("stt", [128, 12])
            S["rs"] = sba("rs", [128, 16])
            S["psE"] = stack.enter_context(nc.psum_tensor(px + "psE", [128, 512], F32))
            S["psA"] = stack.enter_context(nc.psum_tensor(px + "psA", [128, 1024], F32))
            S["psB"] = stack.enter_context(nc.psum_tensor(px + "psB", [128, 1024], F32))
            S["psC"] = stack.enter_context(nc.psum_tensor(px + "psC", [128, 1024], F32))
            S["psD"] = stack.enter_context(nc.psum_tensor(px + "psD", [128, 1024], BF16))
            return S

        def alloc_B(stack, px=""):
            sbb = lambda name, shape, dt=F32: stack.enter_context(nc.sbuf_tensor(px + name, shape, dt))
            S = {}
            S["wgT"] = sbb("wgT", [128, NKI * HID], BF16)
            S["wuT"] = sbb("wuT", [128, NKI * HID], BF16)
            S["wdT"] = sbb("wdT", [128, NHC * D], BF16)
            S["yt"] = [sbb("yt0", [128, 2 * D]), sbb("yt1", [128, 2 * D])]
            S["g"] = sbb("g", [128, 2 * D], BF16)
            S["gT"] = sbb("gT", [128, NKI * TB], BF16)
            S["hh"] = sbb("hh", [128, NHC * TB], BF16)
            S["sg"] = [sbb("sg0", [128, 2 * TB]), sbb("sg1", [128, 2 * TB])]
            S["ffn"] = sbb("ffn", [128, 2 * D])
            S["out_sb"] = sbb("out_sb", [128, 2 * D])
            S["bst"] = sbb("bst", [128, 16])
            S["psT"] = stack.enter_context(nc.psum_tensor(px + "psT", [128, 1024], BF16))
            S["psG"] = [stack.enter_context(nc.psum_tensor(px + "psG0", [128, 512], F32)),
                        stack.enter_context(nc.psum_tensor(px + "psG1", [128, 512], F32))]
            S["psU"] = [stack.enter_context(nc.psum_tensor(px + "psU0", [128, 512], F32)),
                        stack.enter_context(nc.psum_tensor(px + "psU1", [128, 512], F32))]
            S["psDn"] = [stack.enter_context(nc.psum_tensor(px + "psDn0", [128, 512], F32)),
                         stack.enter_context(nc.psum_tensor(px + "psDn1", [128, 512], F32))]
            return S

        m = MockEng()
        def per_engine(emit, S):
            for idx in range(5):
                engines = [None] * 5
                engines[idx] = m
                emit(S, tuple(engines))

        with contextlib.ExitStack() as p0a:
            SA0 = alloc_A(p0a, px="z0")
            per_engine(emit_phaseA, SA0)
        if "B" in phases:
            with contextlib.ExitStack() as p0b:
                SB0 = alloc_B(p0b, px="z0")
                per_engine(emit_phaseB, SB0)

        sems = {n: es.enter_context(nc.semaphore(n)) for n in semnames}
        R.start_pass1(sems)

        nums = sorted(s.num for s in sems.values())
        clear_range = range(nums[0], nums[-1] + 1)
        with nc.Block() as blk0:
            @blk0.gpsimd
            def _(g):
                g.dma_reset(clear_range)
                g.sem_clear(clear_range)

        blk = es.enter_context(nc.Block())

        pa = contextlib.ExitStack()
        SA = alloc_A(pa)

        pb_holder = {}

        def engine_main(which):
            def f(eng):
                engines = [None] * 5
                idx = {"sync": 0, "scalar": 1, "vector": 2, "tensor": 3, "gpsimd": 4}[which]
                engines[idx] = eng
                sem_of = {"scalar": "AC", "vector": "DV"}
                if which in sem_of:
                    R.eng_for_sem[sem_of[which]] = eng
                emit_phaseA(SA, tuple(engines))
                if "B" in phases:
                    emit_phaseB(pb_holder["SB"], tuple(engines))
                elif which == "sync":
                    R.final_wait(eng, "ST")
                    R.final_wait(eng, "WC")
            return f

        pa.close()
        pb = contextlib.ExitStack()
        SB = alloc_B(pb)
        pb_holder["SB"] = SB

        blk.sync(engine_main("sync"))
        blk.gpsimd(engine_main("gpsimd"))
        blk.vector(engine_main("vector"))
        blk.scalar(engine_main("scalar"))
        blk.tensor(engine_main("tensor"))
        assert R.cnt == R.totals, ("pass mismatch", {k: (R.cnt.get(k), R.totals.get(k))
                                   for k in set(R.cnt) | set(R.totals)
                                   if R.cnt.get(k) != R.totals.get(k)})
        pb.close()

    return nc


def make_masks():
    i = np.arange(128)[:, None]
    j = np.arange(256)[None, :]
    band = (j >= i + 1) & (j <= i + 128)
    first = band & (j >= 128)

    def to_T(m):
        mT = m.T.astype(np.float32)
        outm = np.zeros((128, 256), dtype=np.float32)
        outm[:, 0:128] = mT[0:128, :]
        outm[:, 128:256] = mT[128:256, :]
        return outm.astype(ml_dtypes.bfloat16)
    return to_T(band), to_T(first)


def kernel(L0_attn_x, L0_attn_Wq, L0_attn_Wk, L0_attn_Wv, L0_attn_Wo,
           L0_ffn_w_gate, L0_ffn_w_up, L0_ffn_w_down):
    x = np.ascontiguousarray(np.asarray(L0_attn_x, dtype=np.float32)).reshape(32768, D)
    mask_stdT, mask_firstT = make_masks()
    ident = np.eye(128, dtype=np.float32).astype(ml_dtypes.bfloat16)

    nc = build_program(DEBUG_DUMP, PHASES)
    in_maps = []
    for c in range(8):
        r0 = c * TOK
        halo = np.zeros((128, D), dtype=np.float32) if c % 4 == 0 else x[r0 - 128:r0]
        xhc = np.concatenate([halo, x[r0:r0 + TOK]], axis=0)
        xt_t = np.stack([xhc[t * TA:t * TA + TA + 128].reshape(5, 128, D)
                         .transpose(1, 0, 2).reshape(128, 5 * D) for t in range(NTA)])
        in_maps.append(dict(
            xh=np.ascontiguousarray(xt_t),
            wq=np.ascontiguousarray(np.asarray(L0_attn_Wq, dtype=np.float32)),
            wk=np.ascontiguousarray(np.asarray(L0_attn_Wk, dtype=np.float32)),
            wv=np.ascontiguousarray(np.asarray(L0_attn_Wv, dtype=np.float32)),
            wo=np.ascontiguousarray(np.asarray(L0_attn_Wo, dtype=np.float32)),
            wg=np.ascontiguousarray(np.asarray(L0_ffn_w_gate, dtype=np.int32)),
            wu=np.ascontiguousarray(np.asarray(L0_ffn_w_up, dtype=np.int32)),
            wd=np.ascontiguousarray(np.asarray(L0_ffn_w_down, dtype=np.int32)),
            ident_in=ident,
            consts_in=np.concatenate([np.tile(np.array([[EPS, EPS * 127.0 * 127.0, float(np.log(DQ)), 0.0]],
                                                   dtype=np.float32), (128, 1)),
                              np.where(np.arange(128)[:, None] < 64, QSCALE, 0.0).astype(np.float32),
                              np.where(np.arange(128)[:, None] >= 64, QSCALE, 0.0).astype(np.float32),
                              np.zeros((128, 2), np.float32)], axis=1),
            mask_first_in=mask_firstT if c % 4 == 0 else mask_stdT,
            mask_std_in=mask_stdT,
        ))
    res = run_bass_kernel_spmd(nc, in_maps, core_ids=list(range(8)))
    parts = []
    for r in res.results:
        o = np.asarray(r["out"])
        o = o.reshape(NTB, 128, 2, D).transpose(0, 2, 1, 3).reshape(TOK, D)
        parts.append(o)
    return np.concatenate(parts, axis=0).reshape(2, 16384, D)
